# version 17
# baseline (speedup 1.0000x reference)
import math
from contextlib import ExitStack

import numpy as np
import concourse.bass as bass
import concourse.mybir as mybir
from concourse.bass_utils import run_bass_kernel_spmd

F32 = mybir.dt.float32
AF = mybir.ActivationFunctionType
ALU = mybir.AluOpType
AXX = mybir.AxisListType.X

B, T, E = 2, 8192, 1024
H, DK, DV, VV = 8, 128, 256, 2048
FF = 4096
NCORES = 8
N_OWN = 2048
N_PREF = 6144
NCH_OWN = N_OWN // 128
NCH_PREF = N_PREF // 128
PTILES = N_PREF // 512
OTILES = N_OWN // 512
EPS = 1e-5
GN_EPS = 1e-5
SCALE_BASE = 512.0

ENGINES = ("sync", "tensor", "vector", "scalar", "gpsimd")

TAP_ENABLED = set()
DEBUG_S_OUT = False
PARANOID = False
DMA_SLACK = 0


class Buf:
    __slots__ = ("name", "w", "r")

    def __init__(self, name):
        self.name = name
        self.w = None
        self.r = []


class Op:
    __slots__ = ("idx", "eng", "fn", "deps", "dma", "need_inc", "val", "tag",
                 "small", "raw_pub")

    def __init__(self, idx, eng, fn, dma, tag):
        self.idx = idx
        self.eng = eng
        self.fn = fn
        self.deps = []
        self.dma = dma
        self.need_inc = False
        self.val = None
        self.tag = tag
        self.small = False
        self.raw_pub = False


class Em:
    def __init__(self):
        self.ops = []
        self.last = {}
        self.dmacnt = {}

    def op(self, eng, fn, reads=(), writes=(), dma=False, tag="", small=False):
        deps = {}
        for b in reads:
            if b.w is not None:
                deps[b.w.idx] = (b.w, True)
        for b in writes:
            if b.w is not None:
                deps[b.w.idx] = (b.w, True)
            for ro in b.r:
                if ro.idx not in deps:
                    deps[ro.idx] = (ro, False)
        o = Op(len(self.ops), eng, fn, dma, tag)
        o.small = small
        if PARANOID and eng == "tensor":
            for src in ("vector", "scalar"):
                lo = self.last.get(src)
                if lo is not None:
                    deps[lo.idx] = (lo, True)
        o.deps = [(d, nw) for d, nw in deps.values() if d.dma or d.eng != eng]
        if dma:
            self.dmacnt[eng] = self.dmacnt.get(eng, 0) + 16
        self.ops.append(o)
        self.last[eng] = o
        for b in reads:
            b.r.append(o)
        for b in writes:
            b.w = o
            b.r = []
        return o

    def finalize(self):
        for o in self.ops:
            for d, needs_writes in o.deps:
                if not d.dma:
                    d.need_inc = True
                    if needs_writes:
                        d.raw_pub = True
        cnt = {}
        for o in self.ops:
            if o.dma:
                key = ("d", o.eng)
                cnt[key] = cnt.get(key, 0) + 16
                o.val = cnt[key]
            elif o.need_inc:
                key = ("p", o.eng)
                cnt[key] = cnt.get(key, 0) + 1
                o.val = cnt[key]
        return cnt


def _sem_key(op):
    return ("d", op.eng) if op.dma else ("p", op.eng)


def _gammas():
    s, e = math.log(1.0 / 32.0), math.log(1.0 / 512.0)
    return 1.0 - np.exp(np.linspace(s, e, H, dtype=np.float64))


def _xpos_tables(pos):
    pos = np.asarray(pos, dtype=np.float64)
    D = E
    half = D // 2
    scale_vec = (np.arange(0, D, 2, dtype=np.float64) + 0.4 * D) / (1.4 * D)
    inv_freq = 1.0 / 10000.0 ** (np.arange(half, dtype=np.float64) / half)
    sinus = pos[:, None] * inv_freq[None, :]
    sin, cos = np.sin(sinus), np.cos(sinus)
    scq = scale_vec[None, :] ** (pos[:, None] / SCALE_BASE)
    out = []
    for sc in (scq, 1.0 / scq):
        c = np.repeat(cos * sc, 2, axis=-1)
        s = np.repeat(sin * sc, 2, axis=-1)
        bm = s.copy()
        bm[:, 0::2] *= -1.0
        out.append(c.astype(np.float32))
        out.append(bm.astype(np.float32))
    return out


def _host_constants(core):
    t0 = (core % 4) * N_OWN
    g = _gammas()
    p = np.arange(128, dtype=np.float64)
    c = {"ident": np.eye(128, dtype=np.float32)}
    ij = np.arange(128)
    d = ij[None, :] - ij[:, None]
    tri = d >= 0
    mt = np.zeros((128, H * 128), dtype=np.float32)
    for h in range(H):
        mt[:, h * 128:(h + 1) * 128] = np.where(tri, g[h] ** np.maximum(d, 0), 0.0)
    c["maskT"] = mt
    tt = np.arange(t0, t0 + N_OWN, dtype=np.float64)
    rho = (DK ** -0.5) * np.sqrt((1.0 - g[None, :]) /
                                 (1.0 - g[None, :] ** (tt[:, None] + 1.0)))
    rho_sb = np.zeros((128, NCH_OWN * H), dtype=np.float32)
    rho2_sb = np.zeros((128, NCH_OWN * H), dtype=np.float32)
    for ch in range(NCH_OWN):
        for h in range(H):
            col = rho[ch * 128:(ch + 1) * 128, h]
            rho_sb[:, ch * H + h] = col.astype(np.float32)
            rho2_sb[:, ch * H + h] = (col * col).astype(np.float32)
    c["rho"], c["rho2"] = rho_sb, rho2_sb
    c["gpow"] = np.stack([g[h] ** (p + 1.0) for h in range(H)], 1).astype(np.float32)
    c["dec_own"] = np.stack([g[h] ** (127.0 - p) for h in range(H)], 1).astype(np.float32)
    npad_ch = (N_PREF - t0) // 128 if t0 < N_PREF else 0
    dp = np.zeros((128, NCH_PREF * H), dtype=np.float32)
    for pc in range(NCH_PREF):
        flag = 0.0 if pc < npad_ch else 1.0
        cin = pc % 4
        for h in range(H):
            dp[:, pc * H + h] = (flag * g[h] ** (511.0 - (cin * 128.0 + p))
                                 ).astype(np.float32)
    c["dec_pref"] = dp
    cosq, Bq, _, _ = _xpos_tables(np.arange(t0, t0 + N_OWN))
    kpos = np.clip(np.arange(t0 - N_PREF, t0 + N_OWN), 0, T - 1)
    _, _, cosk, Bk = _xpos_tables(kpos)
    c["cosq"], c["Bq"], c["cosk"], c["Bk"] = cosq, Bq, cosk, Bk
    return c


def build_nc():
    nc = bass.Bass("TRN2", target_bir_lowering=False)

    def din(name, shape):
        return nc.dram_tensor(name, list(shape), F32, kind="ExternalInput").ap()

    x_in = din("x", (N_PREF + N_OWN, E))
    w_qkv = din("w_qkv", (E, 4 * E))
    w_g = din("w_g", (E, VV))
    w_p = din("w_p", (VV, E))
    sw1 = din("sw1", (2, E, FF))
    sw2 = din("sw2", (2, FF, E))
    rw1 = din("rw1", (4, E, FF))
    rw2 = din("rw2", (4, FF, E))
    ident_d = din("ident", (128, 128))
    maskT_d = din("maskT", (128, H * 128))
    rho_d = din("rho", (128, NCH_OWN * H))
    rho2_d = din("rho2", (128, NCH_OWN * H))
    gpow_d = din("gpow", (128, H))
    dec_own_d = din("dec_own", (128, H))
    dec_pref_d = din("dec_pref", (128, NCH_PREF * H))
    cosq_d = din("cosq", (N_OWN, E))
    Bq_d = din("Bq", (N_OWN, E))
    cosk_d = din("cosk", (N_PREF + N_OWN, E))
    Bk_d = din("Bk", (N_PREF + N_OWN, E))
    bqkv_d = din("bqkv_bc", (128, 4 * E))
    bg_d = din("bg_bc", (128, VV))
    bp_d = din("bp_bc", (128, E))
    gateb_d = din("gateb_bc", (128, 4))
    gatew_d = din("gatew_sb", (128, 8 * 4))
    sb1c_d = din("sb1c", (128, 2 * 32))
    rb1c_d = din("rb1c", (128, 4 * 32))
    sb2h_d = din("sb2h_bc", (128, E))
    rb2_d = din("rb2_bc", (128, 4 * E))
    magic_d = din("magic", (128, 2))

    out_d = nc.dram_tensor("out", [N_OWN, E], F32, kind="ExternalOutput").ap()
    s_dram = nc.dram_tensor("s_scratch", [N_OWN, E], F32,
                            kind="ExternalOutput" if DEBUG_S_OUT
                            else "Internal").ap()

    em = Em()
    g_np = _gammas()
    g128 = [float(v) for v in g_np ** 128.0]
    g512 = [float(v) for v in g_np ** 512.0]

    sb = {}

    def asb(name, cols):
        sb[name] = nc.alloc_sbuf_tensor(name, [128, cols], F32).ap()
        return sb[name]

    bigA = asb("bigA", 8192)
    bigB = asb("bigB", 8192)
    bigC = asb("bigC", 8192)
    bigD = asb("bigD", 8192)
    med = asb("med", 5120)
    scr = asb("scr", 2560)
    medw = asb("medw", 1024)
    ident = asb("identS", 128)
    maskT = asb("maskTS", H * 128)
    rho_s = asb("rhoS", NCH_OWN * H)
    rho2_s = asb("rho2S", NCH_OWN * H)
    gpow_s = asb("gpowS", H)
    dec_own_s = asb("decOwnS", H)
    dec_pref_s = asb("decPrefS", NCH_PREF * H)
    bg_s = asb("bgS", VV)
    bp_s = asb("bpS", E)
    gateb_s = asb("gatebS", 4)
    gatew_s = asb("gatewS", 32)
    sb1c_s = asb("sb1cS", 64)
    rb1c_s = asb("rb1cS", 128)
    sb2h_s = asb("sb2hS", E)
    magic_s = asb("magicS", 2)
    bias_big = asb("biasBig", 4 * E)
    stats_s = asb("statsS", NCH_OWN * 2)
    cstore_s = asb("cstoreS", NCH_OWN * 4)
    small_s = asb("smallS", 64)
    S_s = med

    pb = [nc.alloc_psum_tensor(f"pb{i}", [128, 512], F32).ap() for i in range(8)]

    BF = {}

    def buf(name):
        if name not in BF:
            BF[name] = Buf(name)
        return BF[name]

    def xtile_ap(c):
        return bigA[:, c * 1024:(c + 1) * 1024]

    def tab_ap(i):
        return bigA[:, 4096 + i * 1024:4096 + (i + 1) * 1024]

    def xnfm_own_ap(d, c):
        return bigB[:, d * 512 + c * 128:d * 512 + (c + 1) * 128]

    def xnfm_pref_ap(c, d):
        return bigD[:, 4096 + c * 1024 + d * 128:4096 + c * 1024 + (d + 1) * 128]

    def xn_tok_ap():
        return med[:, 4096:5120]

    qa_ap = bigB[:, 4096:5120]
    ka_ap = bigB[:, 5120:6144]
    va_ap = bigB[:, 6144:8192]

    def kpre_ap(c):
        return bigB[:, 4096 + c * 1024:4096 + (c + 1) * 1024]

    def vpre_ap(c):
        return bigC[:, c * 2048:(c + 1) * 2048]

    def normed_ap():
        return bigC[:, 0:2048]

    def gated_ap():
        return bigC[:, 2048:4096]

    def gfm_ap(vc):
        return bigC[:, 4096 + vc * 128:4096 + (vc + 1) * 128]

    def stile_ap():
        return bigC[:, 6144:7168]

    def wpiece_ap(slot):
        return bigD[:, slot * 512:(slot + 1) * 512]

    wctr16 = [0]

    def wpiece16_ap(slot):
        return bigD[:, slot * 512:(slot + 1) * 512]

    junk_ap = scr[:, 0:1024]
    t1_ap = scr[:, 0:1024]
    kdec_ap = scr[:, 1024:1152]

    def qfm_par(p):
        return scr[:, 1152 + p * 640:1280 + p * 640]

    def kfm_par(p):
        return scr[:, 1280 + p * 640:1408 + p * 640]

    def ms_par(p):
        return scr[:, 1408 + p * 640:1536 + p * 640]

    def retc_par(p):
        return scr[:, 1536 + p * 640:1792 + p * 640]

    def S_ap(par, h):
        return S_s[:, par * H * DV + h * DV: par * H * DV + (h + 1) * DV]

    WRING = 8
    wctr = [0]
    dqctr = [0]
    sems = {}
    dma_groups = {"sync": [], "scalar": [], "gpsimd": []}

    def wqueue():
        dqctr[0] += 1
        return ("sync", "scalar")[dqctr[0] % 2]

    def dma(dst_ap, src_ap, reads=(), writes=(), tag="", queue=None):
        if queue is None:
            queue = wqueue()

        def fn(e, dst=dst_ap, src=src_ap):
            return e.dma_start(out=dst, in_=src)
        o = em.op(queue, fn, reads=reads, writes=writes, dma=True, tag=tag)
        dma_groups[queue].extend(writes)
        return o

    def dmaflush(queue):
        bufs = dma_groups[queue]
        dma_groups[queue] = []
        if not bufs and em.dmacnt.get(queue, 0) == 0:
            return None
        tgt = em.dmacnt.get(queue, 0)

        def fn(e, tgt=tgt, queue=queue):
            e.wait_ge(sems[("d", queue)], tgt)
            return e.nop()
        return em.op(queue, fn, reads=(), writes=list(bufs), tag="dflush")

    def tedone(bufs):
        return em.op("tensor", lambda e: None, writes=list(bufs), tag="tedone")

    def vop(fn, reads=(), writes=(), tag="", small=False):
        return em.op("vector", fn, reads=reads, writes=writes, tag=tag,
                     small=small)

    def aop(fn, reads=(), writes=(), tag="", small=False):
        return em.op("scalar", fn, reads=reads, writes=writes, tag=tag,
                     small=small)

    def top(fn, reads=(), writes=(), tag=""):
        return em.op("tensor", fn, reads=reads, writes=writes, tag=tag)

    def tap(name, src_ap, read_bufs, cols):
        if name not in TAP_ENABLED:
            return
        t = nc.dram_tensor("tap_" + name, [128, cols], F32,
                           kind="ExternalOutput").ap()
        dma(t, src_ap, reads=list(read_bufs), writes=[buf("tap_" + name)],
            tag="tap", queue="gpsimd")

    for name, dst, src in [
        ("ident", ident, ident_d), ("maskT", maskT, maskT_d),
        ("rho", rho_s, rho_d), ("rho2", rho2_s, rho2_d),
        ("gpow", gpow_s, gpow_d), ("dec_own", dec_own_s, dec_own_d),
        ("dec_pref", dec_pref_s, dec_pref_d), ("bg", bg_s, bg_d),
        ("bp", bp_s, bp_d), ("gateb", gateb_s, gateb_d),
        ("gatew", gatew_s, gatew_d), ("sb1c", sb1c_s, sb1c_d),
        ("rb1c", rb1c_s, rb1c_d), ("sb2h", sb2h_s, sb2h_d),
        ("magic", magic_s, magic_d),
    ]:
        dma(dst[:], src[:], writes=[buf("c_" + name)], tag="cload", queue="sync")
    dma(bias_big[:], bqkv_d[:], writes=[buf("bias_big")], tag="cload", queue="sync")
    dmaflush("sync")

    def rsqrt_vec(r_ap, y_ap, t_ap, bufs, tag="rsqrt"):
        def seedfn(e, r=r_ap, y=y_ap, t=t_ap):
            e.activation(t, y, AF.Ln)
            e.activation(r, t, AF.Exp, scale=-0.5)
            return e.drain()
        aop(seedfn, reads=list(bufs), writes=list(bufs), tag=tag + "_seed")

        def itfn(e, r=r_ap, y=y_ap, t=t_ap):
            for _ in range(3):
                e.tensor_scalar(t, r, r, -0.5, ALU.mult, ALU.mult)
                e.drain()
                e.tensor_scalar(t, t, y, 1.5, ALU.mult, ALU.add)
                e.drain()
                e.tensor_scalar(r, r, t, None, ALU.mult)
                e.drain()
            return None
        vop(itfn, reads=list(bufs), writes=list(bufs), tag=tag)

    def recip_vec(z_ap, s_ap, t_ap, bufs, tag="recipv"):
        def seedfn(e, z=z_ap, s=s_ap, t=t_ap):
            e.activation(t, s, AF.Ln)
            e.activation(z, t, AF.Exp, scale=-1.0)
            return e.drain()
        aop(seedfn, reads=list(bufs), writes=list(bufs), tag=tag + "_seed")

        def itfn(e, z=z_ap, s=s_ap, t=t_ap):
            for _ in range(2):
                e.tensor_scalar(t, z, s, None, ALU.mult)
                e.drain()
                e.tensor_scalar(t, t, -1.0, 2.0, ALU.mult, ALU.add)
                e.drain()
                e.tensor_scalar(z, z, t, None, ALU.mult)
                e.drain()
            return None
        vop(itfn, reads=list(bufs), writes=list(bufs), tag=tag)

    def vreduce(out_ap, in_ap, op, rbufs, wbufs, tag="vred"):
        def fn(e, o=out_ap, i=in_ap, op=op):
            e.tensor_reduce(o, i, AXX, op)
            return e.drain()
        return vop(fn, reads=list(rbufs), writes=list(wbufs), tag=tag)

    def bn_meanvar(in_aps, stat0, out2_ap, rbufs, wbufs, tag="bn"):
        nst = len(in_aps)

        def fn(e):
            for i, ap in enumerate(in_aps):
                e.bn_stats(small_s[:, stat0 + 6 * i:stat0 + 6 * (i + 1)], ap)
            e.drain()
            e.bn_aggr(out2_ap, small_s[:, stat0:stat0 + 6 * nst])
            return e.drain()
        return vop(fn, reads=list(rbufs), writes=list(wbufs) + [buf("small")],
                   tag=tag)

    def ln_chunk(x_ap, xbuf, xn_ap, xnbuf, store_stats_col=None,
                 apply=True):
        sm = buf("small")
        bn_meanvar([x_ap[:, 0:512], x_ap[:, 512:1024]], 0,
                   small_s[:, 12:14], [xbuf], [sm], tag="ln_bn")
        vop(lambda e: e.tensor_scalar(small_s[:, 2:3], small_s[:, 12:13],
                                      -1.0, None, ALU.mult),
            reads=[sm], writes=[sm], small=True)
        vop(lambda e: e.tensor_scalar(small_s[:, 5:6], small_s[:, 13:14],
                                      1.0, EPS, ALU.mult, ALU.add),
            reads=[sm], writes=[sm], small=True)
        rsqrt_vec(small_s[:, 3:4], small_s[:, 5:6], small_s[:, 7:8], [sm])
        if store_stats_col is not None:
            cidx = store_stats_col
            vop(lambda e, c=cidx: e.tensor_copy(stats_s[:, 2 * c:2 * c + 2],
                                                small_s[:, 2:4]),
                reads=[sm], writes=[buf("stats")], small=True)
        if apply:
            vop(lambda e, x=x_ap, xn=xn_ap: e.tensor_scalar(
                xn, x, small_s[:, 2:3], small_s[:, 3:4], ALU.add, ALU.mult),
                reads=[xbuf, sm], writes=[xnbuf], tag="ln_apply")

    def transpose_block(src_ap, src_bufs, dst_ap, dst_buf, bank):
        pbuf = buf(f"pb{bank}")
        top(lambda e, s=src_ap, bank=bank: e.transpose(
            pb[bank][:, 0:128], s, ident[:, 0:128]),
            reads=list(src_bufs) + [buf("c_ident")], writes=[pbuf], tag="tr")
        vop(lambda e, d=dst_ap, bank=bank: e.tensor_copy(d, pb[bank][:, 0:128]),
            reads=[pbuf], writes=[dst_buf], tag="tr_ev")

    def transpose_batch4(srcs, src_bufs, dst_ap, dst_buf, bank):
        pbuf = buf(f"pb{bank}")
        n = len(srcs)
        for j, sap in enumerate(srcs):
            top(lambda e, sap=sap, j=j, bank=bank: e.transpose(
                pb[bank][:, j * 128:(j + 1) * 128], sap, ident[:, 0:128]),
                reads=list(src_bufs) + [buf("c_ident")], writes=[pbuf], tag="tr")
        vop(lambda e, d=dst_ap, n=n, bank=bank: e.tensor_copy(
            d, pb[bank][:, 0:n * 128]),
            reads=[pbuf], writes=[dst_buf], tag="tr_ev")

    def xpos_inplace(y_ap, ybuf, cos_ap, B_ap, tab_bufs):
        tb = buf("scrA")
        sw = y_ap.rearrange("p (a b) -> p a b", b=2)[:, :, ::-1]
        vop(lambda e, sw=sw, Bm=B_ap: e.tensor_tensor(
            t1_ap.rearrange("p (a b) -> p a b", b=2), sw,
            Bm.rearrange("p (a b) -> p a b", b=2), ALU.mult),
            reads=[ybuf] + list(tab_bufs), writes=[tb], tag="xpos1")
        vop(lambda e, y=y_ap, c=cos_ap: e.tensor_tensor(y, y, c, ALU.mult),
            reads=[ybuf] + list(tab_bufs), writes=[ybuf], tag="xpos2")
        vop(lambda e, y=y_ap: e.tensor_tensor(y, y, t1_ap, ALU.add),
            reads=[tb, ybuf], writes=[ybuf], tag="xpos3")

    state_par = [0]

    def state_combine(h, pbank, gpow_chunk):
        pi = state_par[0]
        po = 1 - pi
        vop(lambda e, pi=pi, po=po, h=h, g=gpow_chunk: e.scalar_tensor_tensor(
            S_ap(po, h), S_ap(pi, h), g, pb[pbank][:, 0:256], ALU.mult, ALU.add),
            reads=[buf(f"S{pi}_{h}"), buf(f"pb{pbank}")],
            writes=[buf(f"S{po}_{h}")], tag="s_comb")

    for h in range(H):
        vop(lambda e, h=h: e.memset(S_ap(0, h), 0.0), writes=[buf(f"S0_{h}")],
            tag="szero")

    fence_bigC = [buf("rA"), buf("rB"), buf("gfm"), buf("stile"),
                  buf("stileB")] + \
                 [buf(f"vp{c}") for c in range(4)]
    fence_bigB = [buf("xnfm_own")] + [buf(f"xnp{c}") for c in range(4)] + \
                 [buf("qa"), buf("ka"), buf("va")] + [buf(f"kp{c}") for c in range(4)]
    fence_bigA = [buf(f"bigA{c}") for c in range(4)] + \
                 [buf("tabA"), buf("tabB"), buf("tabC"), buf("tabD")]
    fence_bigD = [buf(f"wbuf{s}") for s in range(WRING)] + \
                 [buf(f"xnp{c}") for c in range(4)]
    fences = {"bigA": fence_bigA, "bigB": fence_bigB,
              "bigC": fence_bigC, "bigD": fence_bigD,
              "medwF0": [], "medwF1": [buf("scrA"), buf("tmpg0"), buf("tmpg1")],
              "medw2F0": [buf("c_maskT")],
              "medw2F1": [buf("kdec0"), buf("qfm0"), buf("kfm0"), buf("ms0"),
                          buf("retc0"), buf("qfm1"), buf("kfm1")],
              **{f"wb16F{sl}": [buf(f"xnp{cc}") for cc in range(4)]
                 for sl in range(8, 16)},
              "medS0": [buf(f"S0_{h}") for h in range(H)],
              "medS1": [buf(f"S1_{h}") for h in range(H)],
              "medXG": [buf("xn_tok")]}
    fence_used = set()

    def wfence(region, bufs):
        if region in fence_used:
            return list(bufs)
        fence_used.add(region)
        return list(bufs) + fences[region]


    for pt in range(PTILES):
        base = pt * 512
        for c in range(4):
            dma(xtile_ap(c), x_in[base + c * 128: base + (c + 1) * 128, :],
                writes=[buf(f"bigA{c}")], tag="xld", queue="gpsimd")
        dmaflush("gpsimd")
        for c in range(4):
            ln_chunk(xtile_ap(c), buf(f"bigA{c}"), xn_tok_ap(), buf("xn_tok"))
            for j0, bank in ((0, 4), (4, 5)):
                transpose_batch4(
                    [xn_tok_ap()[:, (j0 + j) * 128:(j0 + j + 1) * 128]
                     for j in range(4)],
                    [buf("xn_tok")],
                    bigD[:, 4096 + c * 1024 + j0 * 128:
                         4096 + c * 1024 + (j0 + 4) * 128],
                    buf(f"xnp{c}"), bank)
        def pref_xpos(c):
            tok0 = base + c * 128
            s0, s1 = (0, 1) if c % 2 == 0 else (2, 3)
            tb = [buf(("tabA", "tabB", "tabC", "tabD")[s0]),
                  buf(("tabA", "tabB", "tabC", "tabD")[s1])]
            q = wqueue()
            dma(tab_ap(s0), cosk_d[tok0:tok0 + 128, :], writes=[tb[0]],
                tag="tabld", queue=q)
            dma(tab_ap(s1), Bk_d[tok0:tok0 + 128, :], writes=[tb[1]],
                tag="tabld", queue=q)
            dmaflush(q)
            xpos_inplace(kpre_ap(c), buf(f"kp{c}"), tab_ap(s0), tab_ap(s1), tb)

        for si in range(6):
            c0 = E + si * 512
            for db in (0, 4):
                q = wqueue()
                slots = []
                for dd in range(4):
                    d = db + dd
                    slot = wctr[0] % WRING
                    wctr[0] += 1
                    slots.append((d, slot))
                    dma(wpiece_ap(slot),
                        w_qkv[d * 128:(d + 1) * 128, c0:c0 + 512],
                        writes=[buf(f"wbuf{slot}")], tag="wld", queue=q)
                dmaflush(q)
                for d, slot in slots:
                    for c in range(4):
                        pbuf = buf(f"pb{c}")
                        top(lambda e, d=d, c=c, slot=slot: e.matmul(
                            pb[c][:, 0:512], xnfm_pref_ap(c, d),
                            wpiece_ap(slot),
                            start=(d == 0), stop=(d == 7)),
                            reads=[buf(f"wbuf{slot}"), buf(f"xnp{c}")],
                            writes=[pbuf], tag="kv_mm")
                tedone([buf(f"wbuf{sl}") for _, sl in slots])
            for c in range(4):
                if si < 2:
                    dst = kpre_ap(c)[:, si * 512:(si + 1) * 512]
                    dbuf = buf(f"kp{c}")
                else:
                    dst = vpre_ap(c)[:, (si - 2) * 512:(si - 1) * 512]
                    dbuf = buf(f"vp{c}")
                pbuf = buf(f"pb{c}")
                vop(lambda e, dst=dst, c=c, c0=c0: e.tensor_tensor(
                    dst, pb[c][:, 0:512], bias_big[:, c0:c0 + 512], ALU.add),
                    reads=[pbuf, buf("bias_big")], writes=[dbuf], tag="kv_ev")
            if si == 1:
                pref_xpos(0)
                pref_xpos(1)
            elif si == 3:
                pref_xpos(2)
                pref_xpos(3)
        tedone([buf(f"xnp{c}") for c in range(4)])
        for h in range(H):
            pbk = 6 + h % 2
            pbuf = buf(f"pb{pbk}")
            for c in range(4):
                pc = pt * 4 + c
                kdp = kdec_ap if c % 2 == 0 else scr[:, 2432:2560]
                kdb = buf(f"kdec{c % 2}")
                dcol = dec_pref_s[:, pc * H + h: pc * H + h + 1]
                vop(lambda e, c=c, h=h, dcol=dcol, kdp=kdp: e.tensor_scalar(
                    kdp, kpre_ap(c)[:, h * 128:(h + 1) * 128], dcol, None,
                    ALU.mult),
                    reads=[buf(f"kp{c}"), buf("c_dec_pref")], writes=[kdb],
                    tag="kdec")
                top(lambda e, c=c, h=h, kdp=kdp, pbk=pbk: e.matmul(
                    pb[pbk][:, 0:256], kdp,
                    vpre_ap(c)[:, h * 256:(h + 1) * 256],
                    start=(c == 0), stop=(c == 3)),
                    reads=[kdb, buf(f"vp{c}")], writes=[pbuf], tag="sup_mm")
            state_combine(h, pbk, g512[h])
        tedone([buf(f"kp{c}") for c in range(4)] + [buf(f"vp{c}") for c in range(4)])
        state_par[0] = 1 - state_par[0]

    def gate_softmax_top2(ch, gate_pbuf):
        sm = buf("small")
        vop(lambda e: e.tensor_tensor(small_s[:, 8:12], pb[6][:, 0:4],
                                      gateb_s[:, 0:4], ALU.add),
            reads=[gate_pbuf, buf("c_gateb")], writes=[sm], tag="gate_bias", small=True)
        vreduce(small_s[:, 12:13], small_s[:, 8:12], ALU.max, [sm], [sm])
        vop(lambda e: e.tensor_scalar(small_s[:, 12:13], small_s[:, 12:13],
                                      -1.0, None, ALU.mult),
            reads=[sm], writes=[sm], small=True)
        aop(lambda e: e.activation(small_s[:, 8:12], small_s[:, 8:12], AF.Exp,
                                   bias=small_s[:, 12:13]),
            reads=[sm], writes=[sm], tag="gate_exp")
        vreduce(small_s[:, 13:14], small_s[:, 8:12], ALU.add, [sm], [sm])
        recip_vec(small_s[:, 14:15], small_s[:, 13:14], small_s[:, 15:16], [sm])
        vop(lambda e: e.tensor_scalar(small_s[:, 8:12], small_s[:, 8:12],
                                      small_s[:, 14:15], None, ALU.mult),
            reads=[sm], writes=[sm], tag="gate_w", small=True)
        for ecol in range(4):
            first = True
            for j in range(4):
                if j == ecol:
                    continue
                vop(lambda e, j=j, ecol=ecol: e.tensor_tensor(
                    small_s[:, 16:17], small_s[:, 8 + j:9 + j],
                    small_s[:, 8 + ecol:9 + ecol], ALU.is_gt),
                    reads=[sm], writes=[sm], small=True)
                if first:
                    vop(lambda e, ecol=ecol: e.tensor_copy(
                        small_s[:, 20 + ecol:21 + ecol], small_s[:, 16:17]),
                        reads=[sm], writes=[sm], small=True)
                    first = False
                else:
                    vop(lambda e, ecol=ecol: e.tensor_tensor(
                        small_s[:, 20 + ecol:21 + ecol],
                        small_s[:, 20 + ecol:21 + ecol], small_s[:, 16:17],
                        ALU.add),
                        reads=[sm], writes=[sm], small=True)
                if j < ecol:
                    vop(lambda e, j=j, ecol=ecol: e.tensor_tensor(
                        small_s[:, 16:17], small_s[:, 8 + j:9 + j],
                        small_s[:, 8 + ecol:9 + ecol], ALU.is_equal),
                        reads=[sm], writes=[sm], small=True)
                    vop(lambda e, ecol=ecol: e.tensor_tensor(
                        small_s[:, 20 + ecol:21 + ecol],
                        small_s[:, 20 + ecol:21 + ecol], small_s[:, 16:17],
                        ALU.add),
                        reads=[sm], writes=[sm], small=True)
            vop(lambda e, ecol=ecol: e.tensor_scalar(
                small_s[:, 20 + ecol:21 + ecol], small_s[:, 20 + ecol:21 + ecol],
                2.0, None, ALU.is_lt),
                reads=[sm], writes=[sm], small=True)
            vop(lambda e, ecol=ecol, ch=ch: e.tensor_tensor(
                cstore_s[:, ch * 4 + ecol:ch * 4 + ecol + 1],
                small_s[:, 8 + ecol:9 + ecol], small_s[:, 20 + ecol:21 + ecol],
                ALU.mult),
                reads=[sm], writes=[buf("cstore")], tag="gate_c", small=True)
        if ch == 0:
            tap("gate_small", small_s[:, 0:32], [sm], 32)

    qB_ap = bigA[:, 0:1024]
    kB_ap = bigA[:, 1024:2048]
    vB_ap = bigA[:, 2048:4096]
    rA_ap = bigC[:, 0:2048]
    rB_ap = bigC[:, 2048:4096]
    stileB_ap = bigC[:, 7168:8192]
    tmpg = (scr[:, 0:512], scr[:, 512:1024])

    def retention_chunk(qa_r, ka_r, va_r, qbuf, kbuf, vbuf_fn, normed_r, nbuf):
        pi = state_par[0]
        for h in range(H):
            p = h % 2
            bsc = 6 if p == 0 else 3
            bic = 7 if p == 0 else 2
            bsu = 4 if p == 0 else 5
            qh = qa_r[:, h * 128:(h + 1) * 128]
            kh = ka_r[:, h * 128:(h + 1) * 128]
            vh = va_r[:, h * 256:(h + 1) * 256]
            vbh = vbuf_fn(h)
            qfm_ap = qfm_par(p)
            kfm_ap = kfm_par(p)
            ms_ap = ms_par(p)
            retc_ap = retc_par(p)
            qfmb, kfmb = buf(f"qfm{p}"), buf(f"kfm{p}")
            msb, retcb = buf(f"ms{p}"), buf(f"retc{p}")
            pbsc, pbic, pbsu = buf(f"pb{bsc}"), buf(f"pb{bic}"), buf(f"pb{bsu}")
            top(lambda e, qh=qh, bsc=bsc: e.transpose(
                pb[bsc][:, 128:256], qh, ident[:, 0:128]),
                reads=[qbuf, buf("c_ident")], writes=[pbsc], tag="qtr")
            top(lambda e, kh=kh, bsc=bsc: e.transpose(
                pb[bsc][:, 256:384], kh, ident[:, 0:128]),
                reads=[kbuf, buf("c_ident")], writes=[pbsc], tag="ktr")
            vop(lambda e, bsc=bsc: e.tensor_copy(
                scr[:, 1152 + (bsc == 3) * 640:1408 + (bsc == 3) * 640],
                pb[bsc][:, 128:384]),
                reads=[pbsc], writes=[qfmb, kfmb], tag="qktr_ev")
            top(lambda e, bsc=bsc, qfm_ap=qfm_ap, kfm_ap=kfm_ap: e.matmul(
                pb[bsc][:, 0:128], kfm_ap, qfm_ap, start=True, stop=True),
                reads=[kfmb, qfmb], writes=[pbsc], tag="sc_mm")
            vop(lambda e, h=h, bsc=bsc, ms_ap=ms_ap: e.tensor_tensor(
                ms_ap, pb[bsc][:, 0:128], maskT[:, h * 128:(h + 1) * 128],
                ALU.mult),
                reads=[pbsc, buf("c_maskT")], writes=[msb], tag="mask")
            top(lambda e, vh=vh, bic=bic, ms_ap=ms_ap: e.matmul(
                pb[bic][:, 0:256], ms_ap, vh, start=True, stop=True),
                reads=[msb, vbh], writes=[pbic], tag="intra_mm")
            top(lambda e, h=h, pi=pi, bic=bic, qfm_ap=qfm_ap: e.matmul(
                pb[bic][:, 256:512], qfm_ap, S_ap(pi, h), start=True, stop=True),
                reads=[qfmb, buf(f"S{pi}_{h}")], writes=[pbic],
                tag="cross_mm")
            vop(lambda e, h=h, bic=bic, retc_ap=retc_ap: e.tensor_scalar(
                retc_ap, pb[bic][:, 256:512], gpow_s[:, h:h + 1], None,
                ALU.mult),
                reads=[pbic, buf("c_gpow")], writes=[retcb], tag="crossg")
            vop(lambda e, h=h, bic=bic, retc_ap=retc_ap, normed_r=normed_r:
                e.tensor_tensor(
                    normed_r[:, h * 256:(h + 1) * 256], retc_ap,
                    pb[bic][:, 0:256], ALU.add),
                reads=[retcb, pbic], writes=[nbuf], tag="retsum")
            kdp = scr[:, 1024:1152] if p == 0 else scr[:, 2432:2560]
            kdb = buf(f"kdec{p}")
            vop(lambda e, kh=kh, h=h, kdp=kdp: e.tensor_scalar(
                kdp, kh, dec_own_s[:, h:h + 1], None, ALU.mult),
                reads=[kbuf, buf("c_dec_own")], writes=[kdb],
                tag="kdec")
            top(lambda e, vh=vh, bsu=bsu, kdp=kdp: e.matmul(
                pb[bsu][:, 0:256], kdp, vh, start=True, stop=True),
                reads=[kdb, vbh], writes=[pbsu], tag="sup_mm")
            state_combine(h, bsu, g128[h])
        state_par[0] = 1 - state_par[0]

    def gn_chunk(ch, normed_r, nbuf):
        sm = buf("small")

        def gn_stats(e):
            for h in range(H):
                e.bn_stats(small_s[:, h * 6:(h + 1) * 6],
                           normed_r[:, h * 256:(h + 1) * 256])
            e.drain()
            for h in range(H):
                e.bn_aggr(small_s[:, 48 + 2 * h:50 + 2 * h],
                          small_s[:, h * 6:(h + 1) * 6])
            return e.drain()
        vop(gn_stats, reads=[nbuf], writes=[sm], tag="gn_bn")
        means8 = small_s[:, 48:64:2]
        vars8 = small_s[:, 49:64:2]
        y8 = small_s[:, 16:24]
        t8 = small_s[:, 24:32]
        st8 = small_s[:, 32:40]
        vop(lambda e: e.tensor_scalar(means8, means8, -1.0, None, ALU.mult),
            reads=[sm], writes=[sm], small=True)
        vop(lambda e, ch=ch: e.tensor_tensor(
            y8, vars8, rho2_s[:, ch * H:(ch + 1) * H], ALU.mult),
            reads=[sm, buf("c_rho2")], writes=[sm], small=True)
        vop(lambda e: e.tensor_scalar(y8, y8, 1.0, GN_EPS, ALU.mult, ALU.add),
            reads=[sm], writes=[sm], small=True)

        def gn_rsqrt_seed(e):
            e.activation(t8, y8, AF.Ln)
            e.activation(st8, t8, AF.Exp, scale=-0.5)
            return e.drain()
        aop(gn_rsqrt_seed, reads=[sm], writes=[sm], tag="gn_seed")

        def gn_rsqrt_nr(e):
            for _ in range(3):
                e.tensor_tensor(t8, st8, st8, ALU.mult)
                e.drain()
                e.tensor_tensor(t8, t8, y8, ALU.mult)
                e.drain()
                e.tensor_scalar(t8, t8, -0.5, 1.5, ALU.mult, ALU.add)
                e.drain()
                e.tensor_tensor(st8, st8, t8, ALU.mult)
                e.drain()
            return None
        vop(gn_rsqrt_nr, reads=[sm], writes=[sm], tag="gn_nr")
        vop(lambda e, ch=ch: e.tensor_tensor(
            st8, st8, rho_s[:, ch * H:(ch + 1) * H], ALU.mult),
            reads=[sm, buf("c_rho")], writes=[sm], small=True)
        for h in range(H):
            vop(lambda e, h=h, normed_r=normed_r: e.tensor_scalar(
                normed_r[:, h * 256:(h + 1) * 256],
                normed_r[:, h * 256:(h + 1) * 256],
                small_s[:, 48 + 2 * h:49 + 2 * h],
                small_s[:, 32 + h:33 + h], ALU.add, ALU.mult),
                reads=[nbuf, sm], writes=[nbuf],
                tag="gn_apply")

    def xpos_tabs(tok_q, tok_k):
        tb = [buf("tabA"), buf("tabB"), buf("tabC"), buf("tabD")]
        q = wqueue()
        dma(tab_ap(0), cosq_d[tok_q:tok_q + 128, :], writes=[tb[0]],
            tag="tabld", queue=q)
        dma(tab_ap(1), Bq_d[tok_q:tok_q + 128, :], writes=[tb[1]],
            tag="tabld", queue=q)
        dma(tab_ap(2), cosk_d[tok_k:tok_k + 128, :], writes=[tb[2]],
            tag="tabld", queue=q)
        dma(tab_ap(3), Bk_d[tok_k:tok_k + 128, :], writes=[tb[3]],
            tag="tabld", queue=q)
        dmaflush(q)
        return tb

    def pair_stats(pr):
        chA, chB = 2 * pr, 2 * pr + 1
        base = N_PREF + pr * 256
        dma(qB_ap, x_in[base:base + 128, :],
            writes=[buf("bigA0")], tag="xld", queue="gpsimd")
        dma(kB_ap, x_in[base + 128:base + 256, :],
            writes=[buf("bigA1")], tag="xld", queue="gpsimd")
        dmaflush("gpsimd")
        for ch, xr, xb in ((chA, qB_ap, "bigA0"), (chB, kB_ap, "bigA1")):
            ln_chunk(xr, buf(xb), None, None, store_stats_col=ch, apply=False)

    def pair_fmt(pr):
        chA, chB = 2 * pr, 2 * pr + 1
        cA, cB = chA % 4, chB % 4
        xnfm_b = buf("xnfm_own")
        for ch, c, xr, xb in ((chA, cA, qB_ap, "bigA0"),
                              (chB, cB, kB_ap, "bigA1")):
            vop(lambda e, xr=xr, ch=ch: e.tensor_scalar(
                xn_tok_ap(), xr, stats_s[:, 2 * ch:2 * ch + 1],
                stats_s[:, 2 * ch + 1:2 * ch + 2], ALU.add, ALU.mult),
                reads=[buf(xb), buf("stats")], writes=[buf("xn_tok")],
                tag="ln_apply")
            for j in range(8):
                transpose_block(xn_tok_ap()[:, j * 128:(j + 1) * 128],
                                [buf("xn_tok")],
                                xnfm_own_ap(j, c), xnfm_b, 4 + j % 2)
            pbuf6 = buf("pb6")
            for d in range(8):
                top(lambda e, d=d, c=c: e.matmul(
                    pb[6][:, 0:4], xnfm_own_ap(d, c),
                    gatew_s[:, d * 4:(d + 1) * 4],
                    start=(d == 0), stop=(d == 7)),
                    reads=[xnfm_b, buf("c_gatew")], writes=[pbuf6],
                    tag="gate_mm")
            gate_softmax_top2(ch, pbuf6)

    def owned_pair(pr):
        chA, chB = 2 * pr, 2 * pr + 1
        cA, cB = chA % 4, chB % 4
        xnfm_b = buf("xnfm_own")
        for si in range(8):
            c0 = si * 512
            pbkA = si % 4
            pbkB = 4 + si % 4
            pbufA, pbufB = buf(f"pb{pbkA}"), buf(f"pb{pbkB}")
            slots = []
            for d in range(8):
                slot = wctr16[0] % 16
                wctr16[0] += 1
                slots.append((d, slot))
                wl = [buf(f"wbuf{slot}")]
                if slot >= 8:
                    wl = wfence(f"wb16F{slot}", wl)
                dma(wpiece16_ap(slot),
                    w_qkv[d * 128:(d + 1) * 128, c0:c0 + 512],
                    writes=wl, tag="wld", queue=("sync", "scalar")[d % 2])
            dmaflush("sync")
            dmaflush("scalar")
            for d, slot in slots:
                top(lambda e, d=d, pbkA=pbkA, slot=slot, cA=cA: e.matmul(
                    pb[pbkA][:, 0:512], xnfm_own_ap(d, cA), wpiece16_ap(slot),
                    start=(d == 0), stop=(d == 7)),
                    reads=[buf(f"wbuf{slot}"), xnfm_b], writes=[pbufA],
                    tag="qkv_mm")
                top(lambda e, d=d, pbkB=pbkB, slot=slot, cB=cB: e.matmul(
                    pb[pbkB][:, 0:512], xnfm_own_ap(d, cB), wpiece16_ap(slot),
                    start=(d == 0), stop=(d == 7)),
                    reads=[buf(f"wbuf{slot}"), xnfm_b], writes=[pbufB],
                    tag="qkv_mmB")
            tedone([buf(f"wbuf{sl}") for _, sl in slots])
            if si < 2:
                dstA, dbA = qa_ap[:, c0:c0 + 512], buf("qa")
                dstB, dbB = qB_ap[:, c0:c0 + 512], buf("bigA0")
            elif si < 4:
                dstA, dbA = ka_ap[:, c0 - 1024:c0 - 512], buf("ka")
                dstB, dbB = kB_ap[:, c0 - 1024:c0 - 512], buf("bigA1")
            else:
                dstA, dbA = va_ap[:, c0 - 2048:c0 - 1536], buf("va")
                dstB = vB_ap[:, c0 - 2048:c0 - 1536]
                dbB = buf("bigA2") if si < 6 else buf("bigA3")
            vop(lambda e, dst=dstA, pbkA=pbkA, c0=c0: e.tensor_tensor(
                dst, pb[pbkA][:, 0:512], bias_big[:, c0:c0 + 512], ALU.add),
                reads=[pbufA, buf("bias_big")], writes=[dbA], tag="qkv_ev")
            vop(lambda e, dst=dstB, pbkB=pbkB, c0=c0: e.tensor_tensor(
                dst, pb[pbkB][:, 0:512], bias_big[:, c0:c0 + 512], ALU.add),
                reads=[pbufB, buf("bias_big")], writes=[dbB], tag="qkv_evB")
        tbA = xpos_tabs(chA * 128, N_PREF + chA * 128)
        xpos_inplace(qa_ap, buf("qa"), tab_ap(0), tab_ap(1), tbA[0:2])
        xpos_inplace(ka_ap, buf("ka"), tab_ap(2), tab_ap(3), tbA[2:4])
        tbB = xpos_tabs(chB * 128, N_PREF + chB * 128)
        xpos_inplace(qB_ap, buf("bigA0"), tab_ap(0), tab_ap(1), tbB[0:2])
        xpos_inplace(kB_ap, buf("bigA1"), tab_ap(2), tab_ap(3), tbB[2:4])
        retention_chunk(qa_ap, ka_ap, va_ap, buf("qa"), buf("ka"),
                        lambda h: buf("va"), rA_ap, buf("rA"))
        tedone([buf("qa"), buf("ka"), buf("va")])
        gn_chunk(chA, rA_ap, buf("rA"))
        retention_chunk(qB_ap, kB_ap, vB_ap, buf("bigA0"), buf("bigA1"),
                        lambda h: buf("bigA2") if h < 4 else buf("bigA3"),
                        rB_ap, buf("rB"))
        tedone([buf("bigA0"), buf("bigA1"), buf("bigA2"), buf("bigA3")])
        gn_chunk(chB, rB_ap, buf("rB"))
        if pr < 7:
            pair_stats(pr + 1)
        for si in range(4):
            c0 = si * 512
            pbkA = si % 4
            pbkB = 4 + si % 4
            pbufA, pbufB = buf(f"pb{pbkA}"), buf(f"pb{pbkB}")
            slots = []
            for d in range(8):
                slot = wctr16[0] % 16
                wctr16[0] += 1
                slots.append((d, slot))
                wl = [buf(f"wbuf{slot}")]
                if slot >= 8:
                    wl = wfence(f"wb16F{slot}", wl)
                dma(wpiece16_ap(slot),
                    w_g[d * 128:(d + 1) * 128, c0:c0 + 512],
                    writes=wl, tag="wld", queue=("sync", "scalar")[d % 2])
            dmaflush("sync")
            dmaflush("scalar")
            for d, slot in slots:
                top(lambda e, d=d, pbkA=pbkA, slot=slot, cA=cA: e.matmul(
                    pb[pbkA][:, 0:512], xnfm_own_ap(d, cA), wpiece16_ap(slot),
                    start=(d == 0), stop=(d == 7)),
                    reads=[buf(f"wbuf{slot}"), xnfm_b], writes=[pbufA],
                    tag="wg_mm")
                top(lambda e, d=d, pbkB=pbkB, slot=slot, cB=cB: e.matmul(
                    pb[pbkB][:, 0:512], xnfm_own_ap(d, cB), wpiece16_ap(slot),
                    start=(d == 0), stop=(d == 7)),
                    reads=[buf(f"wbuf{slot}"), xnfm_b], writes=[pbufB],
                    tag="wg_mmB")
            tedone([buf(f"wbuf{sl}") for _, sl in slots])
            for pk, tg, rbuf, r_ap, pbuf in (
                    (pbkA, 0, buf("rA"), rA_ap, pbufA),
                    (pbkB, 1, buf("rB"), rB_ap, pbufB)):
                tg_ap = tmpg[tg]
                tb2 = buf(f"tmpg{tg}")
                vop(lambda e, pk=pk, c0=c0, tg_ap=tg_ap: e.tensor_tensor(
                    tg_ap, pb[pk][:, 0:512], bg_s[:, c0:c0 + 512], ALU.add),
                    reads=[pbuf, buf("c_bg")], writes=[tb2], tag="wg_ev")
                aop(lambda e, tg_ap=tg_ap: e.activation(tg_ap, tg_ap, AF.Silu),
                    reads=[tb2], writes=[tb2], tag="wg_silu")
                vop(lambda e, r_ap=r_ap, c0=c0, tg_ap=tg_ap: e.tensor_tensor(
                    r_ap[:, c0:c0 + 512], tg_ap, r_ap[:, c0:c0 + 512],
                    ALU.mult),
                    reads=[tb2, rbuf], writes=[rbuf], tag="gate_mul")
        tedone([xnfm_b])
        if pr < 7:
            pair_fmt(pr + 1)
        for v0 in range(0, 16, 4):
            transpose_batch4(
                [rA_ap[:, (v0 + j) * 128:(v0 + j + 1) * 128] for j in range(4)],
                [buf("rA")],
                bigC[:, 4096 + v0 * 128:4096 + (v0 + 4) * 128],
                buf("gfm"), 4 + (v0 // 4) % 2)
        for v0 in range(0, 16, 4):
            transpose_batch4(
                [rB_ap[:, (v0 + j) * 128:(v0 + j + 1) * 128] for j in range(4)],
                [buf("rB")],
                rB_ap[:, v0 * 128:(v0 + 4) * 128],
                buf("rB"), 6 + (v0 // 4) % 2)
        for si in range(2):
            c0 = si * 512
            pbufA, pbufB = buf(f"pb{si}"), buf(f"pb{2 + si}")
            for vb in (0, 8):
                slots = []
                for vv in range(8):
                    vc = vb + vv
                    slot = wctr16[0] % 16
                    wctr16[0] += 1
                    slots.append((vc, slot))
                    wl = [buf(f"wbuf{slot}")]
                    if slot >= 8:
                        wl = wfence(f"wb16F{slot}", wl)
                    dma(wpiece16_ap(slot),
                        w_p[vc * 128:(vc + 1) * 128, c0:c0 + 512],
                        writes=wl, tag="wld",
                        queue=("sync", "scalar")[vv % 2])
                dmaflush("sync")
                dmaflush("scalar")
                for vc, slot in slots:
                    top(lambda e, vc=vc, si=si, slot=slot: e.matmul(
                        pb[si][:, 0:512], gfm_ap(vc), wpiece16_ap(slot),
                        start=(vc == 0), stop=(vc == 15)),
                        reads=[buf(f"wbuf{slot}"), buf("gfm")], writes=[pbufA],
                        tag="wp_mm")
                    top(lambda e, vc=vc, si=si, slot=slot: e.matmul(
                        pb[2 + si][:, 0:512],
                        rB_ap[:, vc * 128:(vc + 1) * 128], wpiece16_ap(slot),
                        start=(vc == 0), stop=(vc == 15)),
                        reads=[buf(f"wbuf{slot}"), buf("rB")], writes=[pbufB],
                        tag="wp_mmB")
                tedone([buf(f"wbuf{sl}") for _, sl in slots])
            vop(lambda e, si=si, c0=c0: e.tensor_tensor(
                stile_ap()[:, c0:c0 + 512], pb[si][:, 0:512],
                bp_s[:, c0:c0 + 512], ALU.add),
                reads=[pbufA, buf("c_bp")], writes=[buf("stile")], tag="wp_ev")
            vop(lambda e, si=si, c0=c0: e.tensor_tensor(
                stileB_ap[:, c0:c0 + 512], pb[2 + si][:, 0:512],
                bp_s[:, c0:c0 + 512], ALU.add),
                reads=[pbufB, buf("c_bp")], writes=[buf("stileB")],
                tag="wp_evB")
        tedone([buf("gfm"), buf("rB")])
        dma(s_dram[chA * 128:(chA + 1) * 128, :], stile_ap(),
            reads=[buf("stile")], writes=[buf("sdram")], tag="s_st",
            queue="gpsimd")
        dma(s_dram[chB * 128:(chB + 1) * 128, :], stileB_ap,
            reads=[buf("stileB")], writes=[buf("sdram")], tag="s_st",
            queue="gpsimd")

    pair_stats(0)
    pair_fmt(0)
    for pr in range(8):
        owned_pair(pr)

    dma(bias_big[:], rb2_d[:], writes=[buf("bias_big")], tag="rb2ld",
        queue="sync")
    dmaflush("sync")

    def xnfm_half_ap(d, t0, n):
        return bigA[:, d * 1024 + t0:d * 1024 + t0 + n]

    def h1q_ap(f1c, t0, n):
        return bigB[:, f1c * 1024 + t0:f1c * 1024 + t0 + n]

    def w2q_ap(f1c):
        return bigC[:, f1c * 1024:(f1c + 1) * 1024]

    def gacc_ap(tc, c0, n):
        return bigD[:, tc * 1024 + c0:tc * 1024 + c0 + n]

    BF16 = mybir.dt.bfloat16
    medw_r = (medw[:], scr[:, 0:1024])
    medw2_r = (maskT[:, 0:1024], scr[:, 1024:2048])
    xnhi = bigA[:, 0:4096].bitcast(BF16)
    xnlo = bigA[:, 4096:8192].bitcast(BF16)
    h1hi = bigB[:, 0:4096].bitcast(BF16)
    h1lo = bigB[:, 4096:8192].bitcast(BF16)

    def w1hi_ap(ring):
        return med[:, ring * 1024:ring * 1024 + 512].bitcast(BF16)

    def w1lo_ap(ring):
        return med[:, ring * 1024 + 512:(ring + 1) * 1024].bitcast(BF16)

    def w2hi_ap(f1c):
        return bigC[:, f1c * 1024:f1c * 1024 + 512].bitcast(BF16)

    def w2lo_ap(f1c):
        return bigC[:, f1c * 1024 + 512:(f1c + 1) * 1024].bitcast(BF16)

    sld_ap = med[:, 2048:3072]
    xld_ap = med[:, 3072:4096]
    xg_ap = med[:, 4096:5120]

    EXPERTS = [("s", 0, sw1, sw2), ("s", 1, sw1, sw2),
               ("r", 0, rw1, rw2), ("r", 1, rw1, rw2),
               ("r", 2, rw1, rw2), ("r", 3, rw1, rw2)]

    ffn_fence_done = [False]

    def ffn_half(hf):
        tok0 = hf * 1024
        for tc in range(8):
            ch = hf * 8 + tc
            dma(xld_ap, x_in[N_PREF + tok0 + tc * 128:
                             N_PREF + tok0 + (tc + 1) * 128, :],
                writes=wfence("medS1", [buf("xld")]), tag="xld2",
                queue="gpsimd")
            dmaflush("gpsimd")
            vop(lambda e, ch=ch: e.tensor_scalar(
                xg_ap, xld_ap, stats_s[:, 2 * ch:2 * ch + 1],
                stats_s[:, 2 * ch + 1:2 * ch + 2], ALU.add, ALU.mult),
                reads=[buf("xld"), buf("stats")],
                writes=wfence("medXG", [buf("xg")]),
                tag="ln_reapply")
            xnhi3 = xnhi.rearrange("p (j f) -> p j f", f=1024)
            xnlo3 = xnlo.rearrange("p (j f) -> p j f", f=1024)
            first = (tc == 0 and not ffn_fence_done[0])
            if first:
                ffn_fence_done[0] = True
            for j0, pbk in ((0, 4), (4, 5)):
                pbuf = buf(f"pb{pbk}")
                for j in range(4):
                    top(lambda e, j0=j0, j=j, pbk=pbk: e.transpose(
                        pb[pbk][:, j * 128:(j + 1) * 128],
                        xg_ap[:, (j0 + j) * 128:(j0 + j + 1) * 128],
                        ident[:, 0:128]),
                        reads=[buf("xg"), buf("c_ident")], writes=[pbuf],
                        tag="x2tr")
                ps3 = pb[pbk].rearrange("p (j f) -> p j f", f=128)
                vop(lambda e, j0=j0, tc=tc, ps3=ps3: e.tensor_copy(
                    xnhi3[:, j0:j0 + 4, tc * 128:(tc + 1) * 128], ps3),
                    reads=[pbuf],
                    writes=wfence("bigA", [buf("xnA")])
                    if (first and j0 == 0) else [buf("xnA")],
                    tag="x2tr_ev")
                vop(lambda e, j0=j0, tc=tc, ps3=ps3: e.tensor_tensor(
                    xnlo3[:, j0:j0 + 4, tc * 128:(tc + 1) * 128], ps3,
                    xnhi3[:, j0:j0 + 4, tc * 128:(tc + 1) * 128], ALU.subtract),
                    reads=[pbuf, buf("xnA")], writes=[buf("xnA")],
                    tag="x2tr_lo")
        for ei, (kind, en, w1d, w2d) in enumerate(EXPERTS):
            for q in range(4):
                f10 = q * 1024

                def emit_wdma(f1c, f10=f10, en=en, w1d=w1d, w2d=w2d):
                    ring = f1c % 2
                    f1a = f10 + f1c * 128
                    src_w1 = (w1d[en, :, f1a:f1a + 128]
                              .rearrange("(j p) f -> p j f", p=128))
                    qq = wqueue()
                    qq2 = wqueue()
                    mw = medw_r[ring]
                    mw2 = medw2_r[ring]
                    dma(mw.rearrange("p (j f) -> p j f", j=8), src_w1,
                        writes=wfence(f"medwF{ring}", [buf(f"medw{ring}")]),
                        tag="w1ld", queue=qq)
                    dmaflush(qq)
                    dma(mw2, w2d[en, f1a:f1a + 128, :],
                        writes=wfence(f"medw2F{ring}",
                                      [buf(f"medw2{ring}")]),
                        tag="w2ld", queue=qq2)
                    dmaflush(qq2)

                def emit_wsplit(f1c, ei=ei, q=q):
                    ring = f1c % 2
                    w1b = buf(f"w1pc{ring}")
                    w2b = buf(f"w2q{f1c}")
                    mw = medw_r[ring]
                    mw2 = medw2_r[ring]
                    mwb = buf(f"medw{ring}")
                    mw2b = buf(f"medw2{ring}")
                    aop(lambda e, ring=ring, mw=mw: e.activation(
                        w1hi_ap(ring), mw, AF.Identity),
                        reads=[mwb],
                        writes=wfence("medS0", [w1b]), tag="w1hi")
                    vop(lambda e, ring=ring, mw=mw: e.tensor_tensor(
                        w1lo_ap(ring), mw, w1hi_ap(ring), ALU.subtract),
                        reads=[mwb, w1b], writes=[w1b], tag="w1lo")
                    aop(lambda e, f1c=f1c, mw2=mw2: e.activation(
                        w2hi_ap(f1c), mw2, AF.Identity),
                        reads=[mw2b],
                        writes=wfence("bigC", [w2b]) if (ei == 0 and q == 0)
                        else [w2b],
                        tag="w2hi")
                    vop(lambda e, f1c=f1c, mw2=mw2: e.tensor_tensor(
                        w2lo_ap(f1c), mw2, w2hi_ap(f1c), ALU.subtract),
                        reads=[mw2b, w2b], writes=[w2b], tag="w2lo")

                emit_wdma(0)
                emit_wsplit(0)
                for f1c in range(8):
                    ring = f1c % 2
                    w1b = buf(f"w1pc{ring}")
                    jcol = (f10 + f1c * 128) // 128
                    if f1c < 7:
                        emit_wdma(f1c + 1)
                    if kind == "s":
                        bcol = sb1c_s[:, en * 32 + jcol:en * 32 + jcol + 1]
                        bbuf = buf("c_sb1c")
                    else:
                        bcol = rb1c_s[:, en * 32 + jcol:en * 32 + jcol + 1]
                        bbuf = buf("c_rb1c")
                    for th in range(2):
                        t0c = th * 512
                        pbk = th
                        pbuf = buf(f"pb{pbk}")
                        nmm = 24
                        im = 0
                        for d in range(8):
                            for wap, xap in (
                                (w1hi_ap(ring), xnhi),
                                (w1hi_ap(ring), xnlo),
                                (w1lo_ap(ring), xnhi),
                            ):
                                top(lambda e, d=d, wap=wap, xap=xap, t0c=t0c,
                                    pbk=pbk, im=im, nmm=nmm: e.matmul(
                                        pb[pbk][:, 0:512],
                                        wap[:, d * 128:(d + 1) * 128],
                                        xap[:, d * 1024 + t0c:
                                            d * 1024 + t0c + 512],
                                        start=(im == 0), stop=(im == nmm - 1)),
                                    reads=[w1b, buf("xnA")], writes=[pbuf],
                                    tag="h1_mm")
                                im += 1
                        if th == 1 and f1c < 7:
                            emit_wsplit(f1c + 1)
                        tbk = 6 + th
                        tbuf = buf(f"pb{tbk}")
                        aop(lambda e, f1c=f1c, t0c=t0c, pbk=pbk, bcol=bcol:
                            e.activation(
                                h1hi[:, f1c * 1024 + t0c:f1c * 1024 + t0c + 512],
                                pb[pbk][:, 0:512], AF.Relu, bias=bcol),
                            reads=[pbuf, bbuf],
                            writes=wfence("bigB", [buf("h1q")])
                            if (ei == 0 and q == 0 and f1c == 0 and th == 0)
                            else [buf("h1q")],
                            tag="h1_hi")
                        aop(lambda e, pbk=pbk, tbk=tbk, bcol=bcol: e.activation(
                            pb[tbk][:, 0:512], pb[pbk][:, 0:512], AF.Relu,
                            bias=bcol),
                            reads=[pbuf, bbuf], writes=[tbuf], tag="h1_relu")
                        vop(lambda e, f1c=f1c, t0c=t0c, tbk=tbk: e.tensor_tensor(
                            h1lo[:, f1c * 1024 + t0c:f1c * 1024 + t0c + 512],
                            pb[tbk][:, 0:512],
                            h1hi[:, f1c * 1024 + t0c:f1c * 1024 + t0c + 512],
                            ALU.subtract),
                            reads=[tbuf, buf("h1q")], writes=[buf("h1q")],
                            tag="h1_lo")
                    tedone([w1b])
                for tc in range(8):
                    ch = hf * 8 + tc
                    for si in range(2):
                        c0 = si * 512
                        pbk = 2 + si
                        pbuf = buf(f"pb{pbk}")
                        im = 0
                        nmm = 24
                        for f1c in range(8):
                            for hap, wap in (
                                (h1hi, w2hi_ap(f1c)),
                                (h1hi, w2lo_ap(f1c)),
                                (h1lo, w2hi_ap(f1c)),
                            ):
                                top(lambda e, f1c=f1c, tc=tc, c0=c0, pbk=pbk,
                                    hap=hap, wap=wap, im=im, nmm=nmm: e.matmul(
                                        pb[pbk][:, 0:512],
                                        hap[:, f1c * 1024 + tc * 128:
                                            f1c * 1024 + (tc + 1) * 128],
                                        wap[:, c0:c0 + 512],
                                        start=(im == 0), stop=(im == nmm - 1)),
                                    reads=[buf("h1q"), buf(f"w2q{f1c}")],
                                    writes=[pbuf], tag="h2_mm")
                                im += 1
                        gb = buf("gacc")
                        if ei == 0 and q == 0:
                            vop(lambda e, tc=tc, c0=c0, pbk=pbk: e.tensor_scalar(
                                gacc_ap(tc, c0, 512), pb[pbk][:, 0:512],
                                0.5, None, ALU.mult),
                                reads=[pbuf],
                                writes=wfence("bigD", [gb])
                                if (tc == 0 and si == 0) else [gb],
                                tag="h2_init")
                        elif kind == "s":
                            vop(lambda e, tc=tc, c0=c0, pbk=pbk:
                                e.scalar_tensor_tensor(
                                    gacc_ap(tc, c0, 512), pb[pbk][:, 0:512],
                                    0.5, gacc_ap(tc, c0, 512),
                                    ALU.mult, ALU.add),
                                reads=[pbuf, gb], writes=[gb], tag="h2_acc_s")
                        else:
                            ccol = cstore_s[:, ch * 4 + en:ch * 4 + en + 1]
                            vop(lambda e, tc=tc, c0=c0, pbk=pbk, ccol=ccol:
                                e.scalar_tensor_tensor(
                                    gacc_ap(tc, c0, 512), pb[pbk][:, 0:512],
                                    ccol, gacc_ap(tc, c0, 512),
                                    ALU.mult, ALU.add),
                                reads=[pbuf, gb, buf("cstore")], writes=[gb],
                                tag="h2_acc_r")
                tedone([buf(f"w2q{f1c}") for f1c in range(8)] + [buf("h1q")])
        tedone([buf("xnA")])
        for tc in range(8):
            ch = hf * 8 + tc
            gb = buf("gacc")
            for en in range(4):
                ccol = cstore_s[:, ch * 4 + en:ch * 4 + en + 1]
                vop(lambda e, en=en, ccol=ccol: e.tensor_scalar(
                    xg_ap, bias_big[:, en * E:(en + 1) * E], ccol, None,
                    ALU.mult),
                    reads=[buf("bias_big"), buf("cstore")], writes=[buf("xg")],
                    tag="rb2c")
                vop(lambda e, tc=tc: e.tensor_tensor(
                    gacc_ap(tc, 0, E), gacc_ap(tc, 0, E), xg_ap, ALU.add),
                    reads=[buf("xg"), gb], writes=[gb], tag="rb2add")
            vop(lambda e, tc=tc: e.tensor_tensor(
                gacc_ap(tc, 0, E), gacc_ap(tc, 0, E), sb2h_s[:, 0:E], ALU.add),
                reads=[buf("c_sb2h"), gb], writes=[gb], tag="sb2add")
            if hf == 0 and tc in (0, 5):
                tap(f"gpre{tc}", gacc_ap(tc, 0, E), [gb], 1024)
            aop(lambda e, tc=tc: e.activation(
                gacc_ap(tc, 0, E), gacc_ap(tc, 0, E), AF.Silu),
                reads=[gb], writes=[gb], tag="g_silu")
            dma(sld_ap, s_dram[ch * 128:(ch + 1) * 128, :],
                reads=[buf("sdram")], writes=[buf("sld")], tag="s_ld",
                queue="gpsimd")
            dma(xld_ap, x_in[N_PREF + ch * 128:N_PREF + (ch + 1) * 128, :],
                writes=[buf("xld")], tag="xld3", queue="gpsimd")
            dmaflush("gpsimd")
            vop(lambda e, tc=tc: e.tensor_tensor(
                xg_ap, gacc_ap(tc, 0, E), sld_ap, ALU.mult),
                reads=[gb, buf("sld")], writes=[buf("xg")], tag="fin_mul")
            vop(lambda e: e.tensor_tensor(xg_ap, xg_ap, xld_ap, ALU.add),
                reads=[buf("xg"), buf("xld")], writes=[buf("xg")], tag="fin_add")
            dma(out_d[ch * 128:(ch + 1) * 128, :], xg_ap,
                reads=[buf("xg")], writes=[buf("outd")], tag="out_st",
                queue="gpsimd")

    tap("cstore", cstore_s[:, 0:64], [buf("cstore")], 64)
    tap("stats", stats_s[:, 0:32], [buf("stats")], 32)

    for hf in range(2):
        ffn_half(hf)

    counts = em.finalize()
    per_eng = {e: [] for e in ENGINES}
    for o in em.ops:
        per_eng[o.eng].append(o)

    es = ExitStack()
    for key in [("d", "sync"), ("d", "gpsimd"), ("d", "scalar"),
                ("p", "tensor"), ("p", "vector"), ("p", "scalar"),
                ("p", "gpsimd"), ("p", "sync")]:
        sems[key] = es.enter_context(nc.semaphore(f"{key[0]}_{key[1]}"))
    final_dG = counts.get(("d", "gpsimd"), 0)

    with es:
        with nc.Block() as block:
            def make_body(eng_name):
                ops = per_eng[eng_name]

                def body(e):
                    hw = {}
                    for o in ops:
                        for dsel, _nw in o.deps:
                            if dsel.val is None:
                                continue
                            k = _sem_key(dsel)
                            v = dsel.val
                            if dsel.dma and DMA_SLACK:
                                v = min(v + DMA_SLACK, counts.get(k, v))
                            if hw.get(k, -1) >= v:
                                continue
                            hw[k] = v
                            e.wait_ge(sems[k], v)
                        instr = o.fn(e)
                        if o.dma:
                            instr.then_inc(sems[("d", o.eng)], 16)
                        else:
                            if o.small or o.need_inc:
                                instr = e.drain()
                            if o.need_inc:
                                if instr is None:
                                    instr = e.nop()
                                instr.then_inc(sems[("p", o.eng)], 1)
                    if eng_name == "gpsimd" and final_dG:
                        e.wait_ge(sems[("d", "gpsimd")], final_dG)
                return body

            for eng_name in ENGINES:
                getattr(block, eng_name)(make_body(eng_name))

    return nc, counts, len(em.ops)


def kernel(L0_attn_hidden_states, L0_attn_w_qkv, L0_attn_b_qkv, L0_attn_w_g,
           L0_attn_b_g, L0_attn_w_p, L0_attn_b_p, L0_ffn_gate_w, L0_ffn_gate_b,
           L0_ffn_sw1, L0_ffn_sb1, L0_ffn_sw2, L0_ffn_sb2, L0_ffn_rw1,
           L0_ffn_rb1, L0_ffn_rw2, L0_ffn_rb2):
    x = np.ascontiguousarray(np.asarray(L0_attn_hidden_states, dtype=np.float32))
    wqkv = np.ascontiguousarray(np.asarray(L0_attn_w_qkv, dtype=np.float32))
    bqkv = np.asarray(L0_attn_b_qkv, dtype=np.float32)
    wg = np.ascontiguousarray(np.asarray(L0_attn_w_g, dtype=np.float32))
    bg = np.asarray(L0_attn_b_g, dtype=np.float32)
    wp = np.ascontiguousarray(np.asarray(L0_attn_w_p, dtype=np.float32))
    bp = np.asarray(L0_attn_b_p, dtype=np.float32)
    gw = np.asarray(L0_ffn_gate_w, dtype=np.float32)
    gb = np.asarray(L0_ffn_gate_b, dtype=np.float32)
    sw1 = np.ascontiguousarray(np.asarray(L0_ffn_sw1, dtype=np.float32))
    sb1 = np.asarray(L0_ffn_sb1, dtype=np.float32)
    sw2 = np.ascontiguousarray(np.asarray(L0_ffn_sw2, dtype=np.float32))
    sb2 = np.asarray(L0_ffn_sb2, dtype=np.float32)
    rw1 = np.ascontiguousarray(np.asarray(L0_ffn_rw1, dtype=np.float32))
    rb1 = np.asarray(L0_ffn_rb1, dtype=np.float32)
    rw2 = np.ascontiguousarray(np.asarray(L0_ffn_rw2, dtype=np.float32))
    rb2 = np.asarray(L0_ffn_rb2, dtype=np.float32)

    nc, _, _ = build_nc()

    bqkv_bc = np.ascontiguousarray(np.broadcast_to(bqkv[None, :], (128, 4 * E)))
    bg_bc = np.ascontiguousarray(np.broadcast_to(bg[None, :], (128, VV)))
    bp_bc = np.ascontiguousarray(np.broadcast_to(bp[None, :], (128, E)))
    gateb_bc = np.ascontiguousarray(np.broadcast_to(gb[None, :], (128, 4)))
    gatew_sb = np.ascontiguousarray(gw.reshape(8, 128, 4).transpose(1, 0, 2)
                                    .reshape(128, 32))
    sb1c = np.ascontiguousarray(sb1.reshape(2, 32, 128).transpose(2, 0, 1)
                                .reshape(128, 64))
    rb1c = np.ascontiguousarray(rb1.reshape(4, 32, 128).transpose(2, 0, 1)
                                .reshape(128, 128))
    sb2h_bc = np.ascontiguousarray(np.broadcast_to(
        (0.5 * (sb2[0].astype(np.float64) + sb2[1].astype(np.float64)))
        .astype(np.float32)[None, :], (128, E)))
    rb2_bc = np.ascontiguousarray(np.broadcast_to(rb2[None, :, :], (128, 4, E))
                                  .reshape(128, 4 * E))
    magic = np.ascontiguousarray(np.broadcast_to(
        np.array([0x5F3759DF, 0x7EF477D5], dtype=np.uint32)
        .view(np.float32)[None, :], (128, 2)))

    in_maps = []
    for core in range(NCORES):
        b = core // 4
        t0 = (core % 4) * N_OWN
        npad = max(0, N_PREF - t0)
        xpad = np.zeros((npad, E), dtype=np.float32)
        xpre = x[b, max(0, t0 - N_PREF):t0]
        xcore = np.ascontiguousarray(
            np.concatenate([xpad, xpre, x[b, t0:t0 + N_OWN]], axis=0))
        cc = _host_constants(core)
        in_maps.append({
            "x": xcore, "w_qkv": wqkv, "w_g": wg, "w_p": wp,
            "sw1": sw1, "sw2": sw2, "rw1": rw1, "rw2": rw2,
            "ident": cc["ident"], "maskT": cc["maskT"],
            "rho": cc["rho"], "rho2": cc["rho2"], "gpow": cc["gpow"],
            "dec_own": cc["dec_own"], "dec_pref": cc["dec_pref"],
            "cosq": cc["cosq"], "Bq": cc["Bq"],
            "cosk": cc["cosk"], "Bk": cc["Bk"],
            "bqkv_bc": bqkv_bc, "bg_bc": bg_bc, "bp_bc": bp_bc,
            "gateb_bc": gateb_bc, "gatew_sb": gatew_sb,
            "sb1c": sb1c, "rb1c": rb1c, "sb2h_bc": sb2h_bc, "rb2_bc": rb2_bc,
            "magic": magic,
        })

    res = run_bass_kernel_spmd(nc, in_maps, core_ids=list(range(NCORES)))
    out = np.empty((B, T, E), dtype=np.float32)
    for core in range(NCORES):
        b = core // 4
        t0 = (core % 4) * N_OWN
        out[b, t0:t0 + N_OWN] = res.results[core]["out"]
    return out


# version 18
# speedup vs baseline: 1.1071x; 1.1071x over previous
import math
from contextlib import ExitStack

import numpy as np
import concourse.bass as bass
import concourse.mybir as mybir
from concourse.bass_utils import run_bass_kernel_spmd

F32 = mybir.dt.float32
AF = mybir.ActivationFunctionType
ALU = mybir.AluOpType
AXX = mybir.AxisListType.X

B, T, E = 2, 8192, 1024
H, DK, DV, VV = 8, 128, 256, 2048
FF = 4096
NCORES = 8
N_OWN = 2048
N_PREF = 6144
NCH_OWN = N_OWN // 128
NCH_PREF = N_PREF // 128
PTILES = N_PREF // 512
OTILES = N_OWN // 512
EPS = 1e-5
GN_EPS = 1e-5
SCALE_BASE = 512.0

ENGINES = ("sync", "tensor", "vector", "scalar", "gpsimd")

TAP_ENABLED = set()
DEBUG_S_OUT = False
PARANOID = False
DMA_SLACK = 0


class Buf:
    __slots__ = ("name", "w", "r")

    def __init__(self, name):
        self.name = name
        self.w = None
        self.r = []


class Op:
    __slots__ = ("idx", "eng", "fn", "deps", "dma", "need_inc", "val", "tag",
                 "small", "raw_pub")

    def __init__(self, idx, eng, fn, dma, tag):
        self.idx = idx
        self.eng = eng
        self.fn = fn
        self.deps = []
        self.dma = dma
        self.need_inc = False
        self.val = None
        self.tag = tag
        self.small = False
        self.raw_pub = False


class Em:
    def __init__(self):
        self.ops = []
        self.last = {}
        self.dmacnt = {}

    def op(self, eng, fn, reads=(), writes=(), dma=False, tag="", small=False):
        deps = {}
        for b in reads:
            if b.w is not None:
                deps[b.w.idx] = (b.w, True)
        for b in writes:
            if b.w is not None:
                deps[b.w.idx] = (b.w, True)
            for ro in b.r:
                if ro.idx not in deps:
                    deps[ro.idx] = (ro, False)
        o = Op(len(self.ops), eng, fn, dma, tag)
        o.small = small
        if PARANOID and eng == "tensor":
            for src in ("vector", "scalar"):
                lo = self.last.get(src)
                if lo is not None:
                    deps[lo.idx] = (lo, True)
        o.deps = [(d, nw) for d, nw in deps.values() if d.dma or d.eng != eng]
        if dma:
            self.dmacnt[eng] = self.dmacnt.get(eng, 0) + 16
        self.ops.append(o)
        self.last[eng] = o
        for b in reads:
            b.r.append(o)
        for b in writes:
            b.w = o
            b.r = []
        return o

    def finalize(self):
        for o in self.ops:
            for d, needs_writes in o.deps:
                if not d.dma:
                    d.need_inc = True
                    if needs_writes:
                        d.raw_pub = True
        cnt = {}
        for o in self.ops:
            if o.dma:
                key = ("d", o.eng)
                cnt[key] = cnt.get(key, 0) + 16
                o.val = cnt[key]
            elif o.need_inc:
                key = ("p", o.eng)
                cnt[key] = cnt.get(key, 0) + 1
                o.val = cnt[key]
        return cnt


def _sem_key(op):
    return ("d", op.eng) if op.dma else ("p", op.eng)


def _gammas():
    s, e = math.log(1.0 / 32.0), math.log(1.0 / 512.0)
    return 1.0 - np.exp(np.linspace(s, e, H, dtype=np.float64))


def _xpos_tables(pos):
    pos = np.asarray(pos, dtype=np.float64)
    D = E
    half = D // 2
    scale_vec = (np.arange(0, D, 2, dtype=np.float64) + 0.4 * D) / (1.4 * D)
    inv_freq = 1.0 / 10000.0 ** (np.arange(half, dtype=np.float64) / half)
    sinus = pos[:, None] * inv_freq[None, :]
    sin, cos = np.sin(sinus), np.cos(sinus)
    scq = scale_vec[None, :] ** (pos[:, None] / SCALE_BASE)
    out = []
    for sc in (scq, 1.0 / scq):
        c = np.repeat(cos * sc, 2, axis=-1)
        s = np.repeat(sin * sc, 2, axis=-1)
        bm = s.copy()
        bm[:, 0::2] *= -1.0
        out.append(c.astype(np.float32))
        out.append(bm.astype(np.float32))
    return out


def _host_constants(core):
    t0 = (core % 4) * N_OWN
    g = _gammas()
    p = np.arange(128, dtype=np.float64)
    c = {"ident": np.eye(128, dtype=np.float32)}
    ij = np.arange(128)
    d = ij[None, :] - ij[:, None]
    tri = d >= 0
    mt = np.zeros((128, H * 128), dtype=np.float32)
    for h in range(H):
        mt[:, h * 128:(h + 1) * 128] = np.where(tri, g[h] ** np.maximum(d, 0), 0.0)
    c["maskT"] = mt
    tt = np.arange(t0, t0 + N_OWN, dtype=np.float64)
    rho = (DK ** -0.5) * np.sqrt((1.0 - g[None, :]) /
                                 (1.0 - g[None, :] ** (tt[:, None] + 1.0)))
    rho_sb = np.zeros((128, NCH_OWN * H), dtype=np.float32)
    rho2_sb = np.zeros((128, NCH_OWN * H), dtype=np.float32)
    for ch in range(NCH_OWN):
        for h in range(H):
            col = rho[ch * 128:(ch + 1) * 128, h]
            rho_sb[:, ch * H + h] = col.astype(np.float32)
            rho2_sb[:, ch * H + h] = (col * col).astype(np.float32)
    c["rho"], c["rho2"] = rho_sb, rho2_sb
    c["gpow"] = np.stack([g[h] ** (p + 1.0) for h in range(H)], 1).astype(np.float32)
    c["dec_own"] = np.stack([g[h] ** (127.0 - p) for h in range(H)], 1).astype(np.float32)
    npad_ch = (N_PREF - t0) // 128 if t0 < N_PREF else 0
    dp = np.zeros((128, NCH_PREF * H), dtype=np.float32)
    for pc in range(NCH_PREF):
        flag = 0.0 if pc < npad_ch else 1.0
        cin = pc % 4
        for h in range(H):
            dp[:, pc * H + h] = (flag * g[h] ** (511.0 - (cin * 128.0 + p))
                                 ).astype(np.float32)
    c["dec_pref"] = dp
    cosq, Bq, _, _ = _xpos_tables(np.arange(t0, t0 + N_OWN))
    kpos = np.clip(np.arange(t0 - N_PREF, t0 + N_OWN), 0, T - 1)
    _, _, cosk, Bk = _xpos_tables(kpos)
    c["cosq"], c["Bq"], c["cosk"], c["Bk"] = cosq, Bq, cosk, Bk
    return c


def build_nc():
    nc = bass.Bass("TRN2", target_bir_lowering=False)

    def din(name, shape):
        return nc.dram_tensor(name, list(shape), F32, kind="ExternalInput").ap()

    x_in = din("x", (N_PREF + N_OWN, E))
    w_qkv = din("w_qkv", (E, 4 * E))
    w_g = din("w_g", (E, VV))
    w_p = din("w_p", (VV, E))
    sw1 = din("sw1", (2, E, FF))
    sw2 = din("sw2", (2, FF, E))
    rw1 = din("rw1", (4, E, FF))
    rw2 = din("rw2", (4, FF, E))
    ident_d = din("ident", (128, 128))
    maskT_d = din("maskT", (128, H * 128))
    rho_d = din("rho", (128, NCH_OWN * H))
    rho2_d = din("rho2", (128, NCH_OWN * H))
    gpow_d = din("gpow", (128, H))
    dec_own_d = din("dec_own", (128, H))
    dec_pref_d = din("dec_pref", (128, NCH_PREF * H))
    cosq_d = din("cosq", (N_OWN, E))
    Bq_d = din("Bq", (N_OWN, E))
    cosk_d = din("cosk", (N_PREF + N_OWN, E))
    Bk_d = din("Bk", (N_PREF + N_OWN, E))
    bqkv_d = din("bqkv_bc", (128, 4 * E))
    bg_d = din("bg_bc", (128, VV))
    bp_d = din("bp_bc", (128, E))
    gateb_d = din("gateb_bc", (128, 4))
    gatew_d = din("gatew_sb", (128, 8 * 4))
    sb1c_d = din("sb1c", (128, 2 * 32))
    rb1c_d = din("rb1c", (128, 4 * 32))
    sb2h_d = din("sb2h_bc", (128, E))
    rb2_d = din("rb2_bc", (128, 4 * E))
    magic_d = din("magic", (128, 2))

    out_d = nc.dram_tensor("out", [N_OWN, E], F32, kind="ExternalOutput").ap()
    s_dram = nc.dram_tensor("s_scratch", [N_OWN, E], F32,
                            kind="ExternalOutput" if DEBUG_S_OUT
                            else "Internal").ap()

    em = Em()
    g_np = _gammas()
    g128 = [float(v) for v in g_np ** 128.0]
    g512 = [float(v) for v in g_np ** 512.0]

    sb = {}

    def asb(name, cols):
        sb[name] = nc.alloc_sbuf_tensor(name, [128, cols], F32).ap()
        return sb[name]

    bigA = asb("bigA", 8192)
    bigB = asb("bigB", 8192)
    bigC = asb("bigC", 8192)
    bigD = asb("bigD", 8192)
    med = asb("med", 5120)
    scr = asb("scr", 2560)
    medw = asb("medw", 1024)
    ident = asb("identS", 128)
    maskT = asb("maskTS", H * 128)
    rho_s = asb("rhoS", NCH_OWN * H)
    rho2_s = asb("rho2S", NCH_OWN * H)
    gpow_s = asb("gpowS", H)
    dec_own_s = asb("decOwnS", H)
    dec_pref_s = asb("decPrefS", NCH_PREF * H)
    bg_s = asb("bgS", VV)
    bp_s = asb("bpS", E)
    gateb_s = asb("gatebS", 4)
    gatew_s = asb("gatewS", 32)
    sb1c_s = asb("sb1cS", 64)
    rb1c_s = asb("rb1cS", 128)
    sb2h_s = asb("sb2hS", E)
    magic_s = asb("magicS", 2)
    bias_big = asb("biasBig", 4 * E)
    stats_s = asb("statsS", NCH_OWN * 2)
    cstore_s = asb("cstoreS", NCH_OWN * 4)
    small_s = asb("smallS", 64)
    S_s = med

    pb = [nc.alloc_psum_tensor(f"pb{i}", [128, 512], F32).ap() for i in range(8)]

    BF = {}

    def buf(name):
        if name not in BF:
            BF[name] = Buf(name)
        return BF[name]

    def xtile_ap(c):
        return bigA[:, c * 1024:(c + 1) * 1024]

    def tab_ap(i):
        return bigA[:, 4096 + i * 1024:4096 + (i + 1) * 1024]

    def xnfm_own_ap(d, c):
        return bigB[:, d * 512 + c * 128:d * 512 + (c + 1) * 128]

    def xnfm_pref_ap(c, d):
        return bigD[:, 4096 + c * 1024 + d * 128:4096 + c * 1024 + (d + 1) * 128]

    def xn_tok_ap():
        return med[:, 4096:5120]

    qa_ap = bigB[:, 4096:5120]
    ka_ap = bigB[:, 5120:6144]
    va_ap = bigB[:, 6144:8192]

    def kpre_ap(c):
        return bigB[:, 4096 + c * 1024:4096 + (c + 1) * 1024]

    def vpre_ap(c):
        return bigC[:, c * 2048:(c + 1) * 2048]

    def normed_ap():
        return bigC[:, 0:2048]

    def gated_ap():
        return bigC[:, 2048:4096]

    def gfm_ap(vc):
        return bigC[:, 4096 + vc * 128:4096 + (vc + 1) * 128]

    def stile_ap():
        return bigC[:, 6144:7168]

    def wpiece_ap(slot):
        return bigD[:, slot * 512:(slot + 1) * 512]

    wctr16 = [0]

    def wpiece16_ap(slot):
        return bigD[:, slot * 512:(slot + 1) * 512]

    junk_ap = scr[:, 0:1024]
    t1_ap = scr[:, 0:1024]
    kdec_ap = scr[:, 1024:1152]

    def qfm_par(p):
        return scr[:, 1152 + p * 640:1280 + p * 640]

    def kfm_par(p):
        return scr[:, 1280 + p * 640:1408 + p * 640]

    def ms_par(p):
        return scr[:, 1408 + p * 640:1536 + p * 640]

    def retc_par(p):
        return scr[:, 1536 + p * 640:1792 + p * 640]

    def S_ap(par, h):
        return S_s[:, par * H * DV + h * DV: par * H * DV + (h + 1) * DV]

    WRING = 8
    wctr = [0]
    dqctr = [0]
    sems = {}
    dma_groups = {"sync": [], "scalar": [], "gpsimd": []}

    def wqueue():
        dqctr[0] += 1
        return ("sync", "scalar")[dqctr[0] % 2]

    def dma(dst_ap, src_ap, reads=(), writes=(), tag="", queue=None):
        if queue is None:
            queue = wqueue()

        def fn(e, dst=dst_ap, src=src_ap):
            return e.dma_start(out=dst, in_=src)
        o = em.op(queue, fn, reads=reads, writes=writes, dma=True, tag=tag)
        dma_groups[queue].extend(writes)
        return o

    def dmaflush(queue):
        bufs = dma_groups[queue]
        dma_groups[queue] = []
        if not bufs and em.dmacnt.get(queue, 0) == 0:
            return None
        tgt = em.dmacnt.get(queue, 0)

        def fn(e, tgt=tgt, queue=queue):
            e.wait_ge(sems[("d", queue)], tgt)
            return e.nop()
        return em.op(queue, fn, reads=(), writes=list(bufs), tag="dflush")

    def tedone(bufs):
        return em.op("tensor", lambda e: None, writes=list(bufs), tag="tedone")

    def vop(fn, reads=(), writes=(), tag="", small=False):
        return em.op("vector", fn, reads=reads, writes=writes, tag=tag,
                     small=small)

    def aop(fn, reads=(), writes=(), tag="", small=False):
        return em.op("scalar", fn, reads=reads, writes=writes, tag=tag,
                     small=small)

    def top(fn, reads=(), writes=(), tag=""):
        return em.op("tensor", fn, reads=reads, writes=writes, tag=tag)

    def tap(name, src_ap, read_bufs, cols):
        if name not in TAP_ENABLED:
            return
        t = nc.dram_tensor("tap_" + name, [128, cols], F32,
                           kind="ExternalOutput").ap()
        dma(t, src_ap, reads=list(read_bufs), writes=[buf("tap_" + name)],
            tag="tap", queue="gpsimd")

    for name, dst, src in [
        ("ident", ident, ident_d), ("maskT", maskT, maskT_d),
        ("rho", rho_s, rho_d), ("rho2", rho2_s, rho2_d),
        ("gpow", gpow_s, gpow_d), ("dec_own", dec_own_s, dec_own_d),
        ("dec_pref", dec_pref_s, dec_pref_d), ("bg", bg_s, bg_d),
        ("bp", bp_s, bp_d), ("gateb", gateb_s, gateb_d),
        ("gatew", gatew_s, gatew_d), ("sb1c", sb1c_s, sb1c_d),
        ("rb1c", rb1c_s, rb1c_d), ("sb2h", sb2h_s, sb2h_d),
        ("magic", magic_s, magic_d),
    ]:
        dma(dst[:], src[:], writes=[buf("c_" + name)], tag="cload", queue="sync")
    dma(bias_big[:], bqkv_d[:], writes=[buf("bias_big")], tag="cload", queue="sync")
    dmaflush("sync")

    def rsqrt_vec(r_ap, y_ap, t_ap, bufs, tag="rsqrt"):
        def seedfn(e, r=r_ap, y=y_ap, t=t_ap):
            e.activation(t, y, AF.Ln)
            e.activation(r, t, AF.Exp, scale=-0.5)
            return e.drain()
        aop(seedfn, reads=list(bufs), writes=list(bufs), tag=tag + "_seed")

        def itfn(e, r=r_ap, y=y_ap, t=t_ap):
            for _ in range(3):
                e.tensor_scalar(t, r, r, -0.5, ALU.mult, ALU.mult)
                e.drain()
                e.tensor_scalar(t, t, y, 1.5, ALU.mult, ALU.add)
                e.drain()
                e.tensor_scalar(r, r, t, None, ALU.mult)
                e.drain()
            return None
        vop(itfn, reads=list(bufs), writes=list(bufs), tag=tag)

    def recip_vec(z_ap, s_ap, t_ap, bufs, tag="recipv"):
        def seedfn(e, z=z_ap, s=s_ap, t=t_ap):
            e.activation(t, s, AF.Ln)
            e.activation(z, t, AF.Exp, scale=-1.0)
            return e.drain()
        aop(seedfn, reads=list(bufs), writes=list(bufs), tag=tag + "_seed")

        def itfn(e, z=z_ap, s=s_ap, t=t_ap):
            for _ in range(2):
                e.tensor_scalar(t, z, s, None, ALU.mult)
                e.drain()
                e.tensor_scalar(t, t, -1.0, 2.0, ALU.mult, ALU.add)
                e.drain()
                e.tensor_scalar(z, z, t, None, ALU.mult)
                e.drain()
            return None
        vop(itfn, reads=list(bufs), writes=list(bufs), tag=tag)

    def vreduce(out_ap, in_ap, op, rbufs, wbufs, tag="vred"):
        def fn(e, o=out_ap, i=in_ap, op=op):
            e.tensor_reduce(o, i, AXX, op)
            return e.drain()
        return vop(fn, reads=list(rbufs), writes=list(wbufs), tag=tag)

    def bn_meanvar(in_aps, stat0, out2_ap, rbufs, wbufs, tag="bn"):
        nst = len(in_aps)

        def fn(e):
            for i, ap in enumerate(in_aps):
                e.bn_stats(small_s[:, stat0 + 6 * i:stat0 + 6 * (i + 1)], ap)
            e.drain()
            e.bn_aggr(out2_ap, small_s[:, stat0:stat0 + 6 * nst])
            return e.drain()
        return vop(fn, reads=list(rbufs), writes=list(wbufs) + [buf("small")],
                   tag=tag)

    def ln_chunk(x_ap, xbuf, xn_ap, xnbuf, store_stats_col=None):
        sm = buf("small")
        bn_meanvar([x_ap[:, 0:512], x_ap[:, 512:1024]], 0,
                   small_s[:, 12:14], [xbuf], [sm], tag="ln_bn")
        vop(lambda e: e.tensor_scalar(small_s[:, 2:3], small_s[:, 12:13],
                                      -1.0, None, ALU.mult),
            reads=[sm], writes=[sm], small=True)
        vop(lambda e: e.tensor_scalar(small_s[:, 5:6], small_s[:, 13:14],
                                      1.0, EPS, ALU.mult, ALU.add),
            reads=[sm], writes=[sm], small=True)
        rsqrt_vec(small_s[:, 3:4], small_s[:, 5:6], small_s[:, 7:8], [sm])
        if store_stats_col is not None:
            cidx = store_stats_col
            vop(lambda e, c=cidx: e.tensor_copy(stats_s[:, 2 * c:2 * c + 2],
                                                small_s[:, 2:4]),
                reads=[sm], writes=[buf("stats")], small=True)
        vop(lambda e, x=x_ap, xn=xn_ap: e.tensor_scalar(
            xn, x, small_s[:, 2:3], small_s[:, 3:4], ALU.add, ALU.mult),
            reads=[xbuf, sm], writes=[xnbuf], tag="ln_apply")

    def transpose_block(src_ap, src_bufs, dst_ap, dst_buf, bank):
        pbuf = buf(f"pb{bank}")
        top(lambda e, s=src_ap, bank=bank: e.transpose(
            pb[bank][:, 0:128], s, ident[:, 0:128]),
            reads=list(src_bufs) + [buf("c_ident")], writes=[pbuf], tag="tr")
        vop(lambda e, d=dst_ap, bank=bank: e.tensor_copy(d, pb[bank][:, 0:128]),
            reads=[pbuf], writes=[dst_buf], tag="tr_ev")

    def transpose_batch4(srcs, src_bufs, dst_ap, dst_buf, bank):
        pbuf = buf(f"pb{bank}")
        n = len(srcs)
        for j, sap in enumerate(srcs):
            top(lambda e, sap=sap, j=j, bank=bank: e.transpose(
                pb[bank][:, j * 128:(j + 1) * 128], sap, ident[:, 0:128]),
                reads=list(src_bufs) + [buf("c_ident")], writes=[pbuf], tag="tr")
        vop(lambda e, d=dst_ap, n=n, bank=bank: e.tensor_copy(
            d, pb[bank][:, 0:n * 128]),
            reads=[pbuf], writes=[dst_buf], tag="tr_ev")

    def xpos_inplace(y_ap, ybuf, cos_ap, B_ap, tab_bufs):
        tb = buf("scrA")
        sw = y_ap.rearrange("p (a b) -> p a b", b=2)[:, :, ::-1]
        vop(lambda e, sw=sw, Bm=B_ap: e.tensor_tensor(
            t1_ap.rearrange("p (a b) -> p a b", b=2), sw,
            Bm.rearrange("p (a b) -> p a b", b=2), ALU.mult),
            reads=[ybuf] + list(tab_bufs), writes=[tb], tag="xpos1")
        vop(lambda e, y=y_ap, c=cos_ap: e.tensor_tensor(y, y, c, ALU.mult),
            reads=[ybuf] + list(tab_bufs), writes=[ybuf], tag="xpos2")
        vop(lambda e, y=y_ap: e.tensor_tensor(y, y, t1_ap, ALU.add),
            reads=[tb, ybuf], writes=[ybuf], tag="xpos3")

    state_par = [0]

    def state_combine(h, pbank, gpow_chunk):
        pi = state_par[0]
        po = 1 - pi
        vop(lambda e, pi=pi, po=po, h=h, g=gpow_chunk: e.scalar_tensor_tensor(
            S_ap(po, h), S_ap(pi, h), g, pb[pbank][:, 0:256], ALU.mult, ALU.add),
            reads=[buf(f"S{pi}_{h}"), buf(f"pb{pbank}")],
            writes=[buf(f"S{po}_{h}")], tag="s_comb")

    for h in range(H):
        vop(lambda e, h=h: e.memset(S_ap(0, h), 0.0), writes=[buf(f"S0_{h}")],
            tag="szero")

    fence_bigC = [buf("rA"), buf("rB"), buf("gfm"), buf("stile"),
                  buf("stileB")] + \
                 [buf(f"vp{c}") for c in range(4)]
    fence_bigB = [buf("xnfm_own")] + [buf(f"xnp{c}") for c in range(4)] + \
                 [buf("qa"), buf("ka"), buf("va")] + [buf(f"kp{c}") for c in range(4)]
    fence_bigA = [buf(f"bigA{c}") for c in range(4)] + \
                 [buf("tabA"), buf("tabB"), buf("tabC"), buf("tabD")]
    fence_bigD = [buf(f"wbuf{s}") for s in range(WRING)] + \
                 [buf(f"xnp{c}") for c in range(4)]
    fences = {"bigA": fence_bigA, "bigB": fence_bigB,
              "bigC": fence_bigC, "bigD": fence_bigD,
              "medwF0": [], "medwF1": [buf("scrA"), buf("tmpg0"), buf("tmpg1")],
              "medw2F0": [buf("c_maskT")],
              "medw2F1": [buf("kdec0"), buf("qfm0"), buf("kfm0"), buf("ms0"),
                          buf("retc0"), buf("qfm1"), buf("kfm1")],
              **{f"wb16F{sl}": [buf(f"xnp{cc}") for cc in range(4)]
                 for sl in range(8, 16)},
              "medS0": [buf(f"S0_{h}") for h in range(H)],
              "medS1": [buf(f"S1_{h}") for h in range(H)],
              "medXG": [buf("xn_tok")]}
    fence_used = set()

    def wfence(region, bufs):
        if region in fence_used:
            return list(bufs)
        fence_used.add(region)
        return list(bufs) + fences[region]


    for pt in range(PTILES):
        base = pt * 512
        for c in range(4):
            dma(xtile_ap(c), x_in[base + c * 128: base + (c + 1) * 128, :],
                writes=[buf(f"bigA{c}")], tag="xld", queue="gpsimd")
        dmaflush("gpsimd")
        for c in range(4):
            ln_chunk(xtile_ap(c), buf(f"bigA{c}"), xn_tok_ap(), buf("xn_tok"))
            for j0, bank in ((0, 4), (4, 5)):
                transpose_batch4(
                    [xn_tok_ap()[:, (j0 + j) * 128:(j0 + j + 1) * 128]
                     for j in range(4)],
                    [buf("xn_tok")],
                    bigD[:, 4096 + c * 1024 + j0 * 128:
                         4096 + c * 1024 + (j0 + 4) * 128],
                    buf(f"xnp{c}"), bank)
        def pref_xpos(c):
            tok0 = base + c * 128
            s0, s1 = (0, 1) if c % 2 == 0 else (2, 3)
            tb = [buf(("tabA", "tabB", "tabC", "tabD")[s0]),
                  buf(("tabA", "tabB", "tabC", "tabD")[s1])]
            q = wqueue()
            dma(tab_ap(s0), cosk_d[tok0:tok0 + 128, :], writes=[tb[0]],
                tag="tabld", queue=q)
            dma(tab_ap(s1), Bk_d[tok0:tok0 + 128, :], writes=[tb[1]],
                tag="tabld", queue=q)
            dmaflush(q)
            xpos_inplace(kpre_ap(c), buf(f"kp{c}"), tab_ap(s0), tab_ap(s1), tb)

        for si in range(6):
            c0 = E + si * 512
            for db in (0, 4):
                q = wqueue()
                slots = []
                for dd in range(4):
                    d = db + dd
                    slot = wctr[0] % WRING
                    wctr[0] += 1
                    slots.append((d, slot))
                    dma(wpiece_ap(slot),
                        w_qkv[d * 128:(d + 1) * 128, c0:c0 + 512],
                        writes=[buf(f"wbuf{slot}")], tag="wld", queue=q)
                dmaflush(q)
                for d, slot in slots:
                    for c in range(4):
                        pbuf = buf(f"pb{c}")
                        top(lambda e, d=d, c=c, slot=slot: e.matmul(
                            pb[c][:, 0:512], xnfm_pref_ap(c, d),
                            wpiece_ap(slot),
                            start=(d == 0), stop=(d == 7)),
                            reads=[buf(f"wbuf{slot}"), buf(f"xnp{c}")],
                            writes=[pbuf], tag="kv_mm")
                tedone([buf(f"wbuf{sl}") for _, sl in slots])
            for c in range(4):
                if si < 2:
                    dst = kpre_ap(c)[:, si * 512:(si + 1) * 512]
                    dbuf = buf(f"kp{c}")
                else:
                    dst = vpre_ap(c)[:, (si - 2) * 512:(si - 1) * 512]
                    dbuf = buf(f"vp{c}")
                pbuf = buf(f"pb{c}")
                vop(lambda e, dst=dst, c=c, c0=c0: e.tensor_tensor(
                    dst, pb[c][:, 0:512], bias_big[:, c0:c0 + 512], ALU.add),
                    reads=[pbuf, buf("bias_big")], writes=[dbuf], tag="kv_ev")
            if si == 1:
                pref_xpos(0)
                pref_xpos(1)
            elif si == 3:
                pref_xpos(2)
                pref_xpos(3)
        tedone([buf(f"xnp{c}") for c in range(4)])
        for h in range(H):
            pbk = 6 + h % 2
            pbuf = buf(f"pb{pbk}")
            for c in range(4):
                pc = pt * 4 + c
                kdp = kdec_ap if c % 2 == 0 else scr[:, 2432:2560]
                kdb = buf(f"kdec{c % 2}")
                dcol = dec_pref_s[:, pc * H + h: pc * H + h + 1]
                vop(lambda e, c=c, h=h, dcol=dcol, kdp=kdp: e.tensor_scalar(
                    kdp, kpre_ap(c)[:, h * 128:(h + 1) * 128], dcol, None,
                    ALU.mult),
                    reads=[buf(f"kp{c}"), buf("c_dec_pref")], writes=[kdb],
                    tag="kdec")
                top(lambda e, c=c, h=h, kdp=kdp, pbk=pbk: e.matmul(
                    pb[pbk][:, 0:256], kdp,
                    vpre_ap(c)[:, h * 256:(h + 1) * 256],
                    start=(c == 0), stop=(c == 3)),
                    reads=[kdb, buf(f"vp{c}")], writes=[pbuf], tag="sup_mm")
            state_combine(h, pbk, g512[h])
        tedone([buf(f"kp{c}") for c in range(4)] + [buf(f"vp{c}") for c in range(4)])
        state_par[0] = 1 - state_par[0]

    def gate_softmax_top2(ch, gate_pbuf):
        sm = buf("small")
        vop(lambda e: e.tensor_tensor(small_s[:, 8:12], pb[6][:, 0:4],
                                      gateb_s[:, 0:4], ALU.add),
            reads=[gate_pbuf, buf("c_gateb")], writes=[sm], tag="gate_bias", small=True)
        vreduce(small_s[:, 12:13], small_s[:, 8:12], ALU.max, [sm], [sm])
        vop(lambda e: e.tensor_scalar(small_s[:, 12:13], small_s[:, 12:13],
                                      -1.0, None, ALU.mult),
            reads=[sm], writes=[sm], small=True)
        aop(lambda e: e.activation(small_s[:, 8:12], small_s[:, 8:12], AF.Exp,
                                   bias=small_s[:, 12:13]),
            reads=[sm], writes=[sm], tag="gate_exp")
        vreduce(small_s[:, 13:14], small_s[:, 8:12], ALU.add, [sm], [sm])
        recip_vec(small_s[:, 14:15], small_s[:, 13:14], small_s[:, 15:16], [sm])
        vop(lambda e: e.tensor_scalar(small_s[:, 8:12], small_s[:, 8:12],
                                      small_s[:, 14:15], None, ALU.mult),
            reads=[sm], writes=[sm], tag="gate_w", small=True)
        for ecol in range(4):
            first = True
            for j in range(4):
                if j == ecol:
                    continue
                vop(lambda e, j=j, ecol=ecol: e.tensor_tensor(
                    small_s[:, 16:17], small_s[:, 8 + j:9 + j],
                    small_s[:, 8 + ecol:9 + ecol], ALU.is_gt),
                    reads=[sm], writes=[sm], small=True)
                if first:
                    vop(lambda e, ecol=ecol: e.tensor_copy(
                        small_s[:, 20 + ecol:21 + ecol], small_s[:, 16:17]),
                        reads=[sm], writes=[sm], small=True)
                    first = False
                else:
                    vop(lambda e, ecol=ecol: e.tensor_tensor(
                        small_s[:, 20 + ecol:21 + ecol],
                        small_s[:, 20 + ecol:21 + ecol], small_s[:, 16:17],
                        ALU.add),
                        reads=[sm], writes=[sm], small=True)
                if j < ecol:
                    vop(lambda e, j=j, ecol=ecol: e.tensor_tensor(
                        small_s[:, 16:17], small_s[:, 8 + j:9 + j],
                        small_s[:, 8 + ecol:9 + ecol], ALU.is_equal),
                        reads=[sm], writes=[sm], small=True)
                    vop(lambda e, ecol=ecol: e.tensor_tensor(
                        small_s[:, 20 + ecol:21 + ecol],
                        small_s[:, 20 + ecol:21 + ecol], small_s[:, 16:17],
                        ALU.add),
                        reads=[sm], writes=[sm], small=True)
            vop(lambda e, ecol=ecol: e.tensor_scalar(
                small_s[:, 20 + ecol:21 + ecol], small_s[:, 20 + ecol:21 + ecol],
                2.0, None, ALU.is_lt),
                reads=[sm], writes=[sm], small=True)
            vop(lambda e, ecol=ecol, ch=ch: e.tensor_tensor(
                cstore_s[:, ch * 4 + ecol:ch * 4 + ecol + 1],
                small_s[:, 8 + ecol:9 + ecol], small_s[:, 20 + ecol:21 + ecol],
                ALU.mult),
                reads=[sm], writes=[buf("cstore")], tag="gate_c", small=True)
        if ch == 0:
            tap("gate_small", small_s[:, 0:32], [sm], 32)

    qB_ap = bigA[:, 0:1024]
    kB_ap = bigA[:, 1024:2048]
    vB_ap = bigA[:, 2048:4096]
    rA_ap = bigC[:, 0:2048]
    rB_ap = bigC[:, 2048:4096]
    stileB_ap = bigC[:, 7168:8192]
    tmpg = (scr[:, 0:512], scr[:, 512:1024])

    def retention_chunk(qa_r, ka_r, va_r, qbuf, kbuf, vbuf_fn, normed_r, nbuf):
        pi = state_par[0]
        for h in range(H):
            p = h % 2
            bsc = 6 if p == 0 else 3
            bic = 7 if p == 0 else 2
            bsu = 4 if p == 0 else 5
            qh = qa_r[:, h * 128:(h + 1) * 128]
            kh = ka_r[:, h * 128:(h + 1) * 128]
            vh = va_r[:, h * 256:(h + 1) * 256]
            vbh = vbuf_fn(h)
            qfm_ap = qfm_par(p)
            kfm_ap = kfm_par(p)
            ms_ap = ms_par(p)
            retc_ap = retc_par(p)
            qfmb, kfmb = buf(f"qfm{p}"), buf(f"kfm{p}")
            msb, retcb = buf(f"ms{p}"), buf(f"retc{p}")
            pbsc, pbic, pbsu = buf(f"pb{bsc}"), buf(f"pb{bic}"), buf(f"pb{bsu}")
            top(lambda e, qh=qh, bsc=bsc: e.transpose(
                pb[bsc][:, 128:256], qh, ident[:, 0:128]),
                reads=[qbuf, buf("c_ident")], writes=[pbsc], tag="qtr")
            top(lambda e, kh=kh, bsc=bsc: e.transpose(
                pb[bsc][:, 256:384], kh, ident[:, 0:128]),
                reads=[kbuf, buf("c_ident")], writes=[pbsc], tag="ktr")
            vop(lambda e, bsc=bsc: e.tensor_copy(
                scr[:, 1152 + (bsc == 3) * 640:1408 + (bsc == 3) * 640],
                pb[bsc][:, 128:384]),
                reads=[pbsc], writes=[qfmb, kfmb], tag="qktr_ev")
            top(lambda e, bsc=bsc, qfm_ap=qfm_ap, kfm_ap=kfm_ap: e.matmul(
                pb[bsc][:, 0:128], kfm_ap, qfm_ap, start=True, stop=True),
                reads=[kfmb, qfmb], writes=[pbsc], tag="sc_mm")
            vop(lambda e, h=h, bsc=bsc, ms_ap=ms_ap: e.tensor_tensor(
                ms_ap, pb[bsc][:, 0:128], maskT[:, h * 128:(h + 1) * 128],
                ALU.mult),
                reads=[pbsc, buf("c_maskT")], writes=[msb], tag="mask")
            top(lambda e, vh=vh, bic=bic, ms_ap=ms_ap: e.matmul(
                pb[bic][:, 0:256], ms_ap, vh, start=True, stop=True),
                reads=[msb, vbh], writes=[pbic], tag="intra_mm")
            top(lambda e, h=h, pi=pi, bic=bic, qfm_ap=qfm_ap: e.matmul(
                pb[bic][:, 256:512], qfm_ap, S_ap(pi, h), start=True, stop=True),
                reads=[qfmb, buf(f"S{pi}_{h}")], writes=[pbic],
                tag="cross_mm")
            vop(lambda e, h=h, bic=bic, retc_ap=retc_ap: e.tensor_scalar(
                retc_ap, pb[bic][:, 256:512], gpow_s[:, h:h + 1], None,
                ALU.mult),
                reads=[pbic, buf("c_gpow")], writes=[retcb], tag="crossg")
            vop(lambda e, h=h, bic=bic, retc_ap=retc_ap, normed_r=normed_r:
                e.tensor_tensor(
                    normed_r[:, h * 256:(h + 1) * 256], retc_ap,
                    pb[bic][:, 0:256], ALU.add),
                reads=[retcb, pbic], writes=[nbuf], tag="retsum")
            kdp = scr[:, 1024:1152] if p == 0 else scr[:, 2432:2560]
            kdb = buf(f"kdec{p}")
            vop(lambda e, kh=kh, h=h, kdp=kdp: e.tensor_scalar(
                kdp, kh, dec_own_s[:, h:h + 1], None, ALU.mult),
                reads=[kbuf, buf("c_dec_own")], writes=[kdb],
                tag="kdec")
            top(lambda e, vh=vh, bsu=bsu, kdp=kdp: e.matmul(
                pb[bsu][:, 0:256], kdp, vh, start=True, stop=True),
                reads=[kdb, vbh], writes=[pbsu], tag="sup_mm")
            state_combine(h, bsu, g128[h])
        state_par[0] = 1 - state_par[0]

    def gn_chunk(ch, normed_r, nbuf):
        sm = buf("small")

        def gn_stats(e):
            for h in range(H):
                e.bn_stats(small_s[:, h * 6:(h + 1) * 6],
                           normed_r[:, h * 256:(h + 1) * 256])
            e.drain()
            for h in range(H):
                e.bn_aggr(small_s[:, 48 + 2 * h:50 + 2 * h],
                          small_s[:, h * 6:(h + 1) * 6])
            return e.drain()
        vop(gn_stats, reads=[nbuf], writes=[sm], tag="gn_bn")
        means8 = small_s[:, 48:64:2]
        vars8 = small_s[:, 49:64:2]
        y8 = small_s[:, 16:24]
        t8 = small_s[:, 24:32]
        st8 = small_s[:, 32:40]
        vop(lambda e: e.tensor_scalar(means8, means8, -1.0, None, ALU.mult),
            reads=[sm], writes=[sm], small=True)
        vop(lambda e, ch=ch: e.tensor_tensor(
            y8, vars8, rho2_s[:, ch * H:(ch + 1) * H], ALU.mult),
            reads=[sm, buf("c_rho2")], writes=[sm], small=True)
        vop(lambda e: e.tensor_scalar(y8, y8, 1.0, GN_EPS, ALU.mult, ALU.add),
            reads=[sm], writes=[sm], small=True)

        def gn_rsqrt_seed(e):
            e.activation(t8, y8, AF.Ln)
            e.activation(st8, t8, AF.Exp, scale=-0.5)
            return e.drain()
        aop(gn_rsqrt_seed, reads=[sm], writes=[sm], tag="gn_seed")

        def gn_rsqrt_nr(e):
            for _ in range(3):
                e.tensor_tensor(t8, st8, st8, ALU.mult)
                e.drain()
                e.tensor_tensor(t8, t8, y8, ALU.mult)
                e.drain()
                e.tensor_scalar(t8, t8, -0.5, 1.5, ALU.mult, ALU.add)
                e.drain()
                e.tensor_tensor(st8, st8, t8, ALU.mult)
                e.drain()
            return None
        vop(gn_rsqrt_nr, reads=[sm], writes=[sm], tag="gn_nr")
        vop(lambda e, ch=ch: e.tensor_tensor(
            st8, st8, rho_s[:, ch * H:(ch + 1) * H], ALU.mult),
            reads=[sm, buf("c_rho")], writes=[sm], small=True)
        for h in range(H):
            vop(lambda e, h=h, normed_r=normed_r: e.tensor_scalar(
                normed_r[:, h * 256:(h + 1) * 256],
                normed_r[:, h * 256:(h + 1) * 256],
                small_s[:, 48 + 2 * h:49 + 2 * h],
                small_s[:, 32 + h:33 + h], ALU.add, ALU.mult),
                reads=[nbuf, sm], writes=[nbuf],
                tag="gn_apply")

    def xpos_tabs(tok_q, tok_k):
        tb = [buf("tabA"), buf("tabB"), buf("tabC"), buf("tabD")]
        q = wqueue()
        dma(tab_ap(0), cosq_d[tok_q:tok_q + 128, :], writes=[tb[0]],
            tag="tabld", queue=q)
        dma(tab_ap(1), Bq_d[tok_q:tok_q + 128, :], writes=[tb[1]],
            tag="tabld", queue=q)
        dma(tab_ap(2), cosk_d[tok_k:tok_k + 128, :], writes=[tb[2]],
            tag="tabld", queue=q)
        dma(tab_ap(3), Bk_d[tok_k:tok_k + 128, :], writes=[tb[3]],
            tag="tabld", queue=q)
        dmaflush(q)
        return tb

    def pair_prologue(pr):
        chA, chB = 2 * pr, 2 * pr + 1
        cA, cB = chA % 4, chB % 4
        base = N_PREF + pr * 256
        dma(qB_ap, x_in[base:base + 128, :],
            writes=[buf("bigA0")], tag="xld", queue="gpsimd")
        dma(kB_ap, x_in[base + 128:base + 256, :],
            writes=[buf("bigA1")], tag="xld", queue="gpsimd")
        dmaflush("gpsimd")
        xnfm_b = buf("xnfm_own")
        for ch, c, xr, xb in ((chA, cA, qB_ap, "bigA0"),
                              (chB, cB, kB_ap, "bigA1")):
            ln_chunk(xr, buf(xb), xn_tok_ap(), buf("xn_tok"),
                     store_stats_col=ch)
            for j in range(8):
                transpose_block(xn_tok_ap()[:, j * 128:(j + 1) * 128],
                                [buf("xn_tok")],
                                xnfm_own_ap(j, c), xnfm_b, 4 + j % 2)
            pbuf6 = buf("pb6")
            for d in range(8):
                top(lambda e, d=d, c=c: e.matmul(
                    pb[6][:, 0:4], xnfm_own_ap(d, c),
                    gatew_s[:, d * 4:(d + 1) * 4],
                    start=(d == 0), stop=(d == 7)),
                    reads=[xnfm_b, buf("c_gatew")], writes=[pbuf6],
                    tag="gate_mm")
            gate_softmax_top2(ch, pbuf6)

    def owned_pair(pr):
        chA, chB = 2 * pr, 2 * pr + 1
        cA, cB = chA % 4, chB % 4
        xnfm_b = buf("xnfm_own")
        for si in range(8):
            c0 = si * 512
            pbkA = si % 4
            pbkB = 4 + si % 4
            pbufA, pbufB = buf(f"pb{pbkA}"), buf(f"pb{pbkB}")
            slots = []
            for d in range(8):
                slot = wctr16[0] % 16
                wctr16[0] += 1
                slots.append((d, slot))
                wl = [buf(f"wbuf{slot}")]
                if slot >= 8:
                    wl = wfence(f"wb16F{slot}", wl)
                dma(wpiece16_ap(slot),
                    w_qkv[d * 128:(d + 1) * 128, c0:c0 + 512],
                    writes=wl, tag="wld",
                    queue=("sync", "scalar", "gpsimd")[d % 3])
            dmaflush("sync")
            dmaflush("scalar")
            dmaflush("gpsimd")
            for d, slot in slots:
                top(lambda e, d=d, pbkA=pbkA, slot=slot, cA=cA: e.matmul(
                    pb[pbkA][:, 0:512], xnfm_own_ap(d, cA), wpiece16_ap(slot),
                    start=(d == 0), stop=(d == 7)),
                    reads=[buf(f"wbuf{slot}"), xnfm_b], writes=[pbufA],
                    tag="qkv_mm")
                top(lambda e, d=d, pbkB=pbkB, slot=slot, cB=cB: e.matmul(
                    pb[pbkB][:, 0:512], xnfm_own_ap(d, cB), wpiece16_ap(slot),
                    start=(d == 0), stop=(d == 7)),
                    reads=[buf(f"wbuf{slot}"), xnfm_b], writes=[pbufB],
                    tag="qkv_mmB")
            tedone([buf(f"wbuf{sl}") for _, sl in slots])
            if si < 2:
                dstA, dbA = qa_ap[:, c0:c0 + 512], buf("qa")
                dstB, dbB = qB_ap[:, c0:c0 + 512], buf("bigA0")
            elif si < 4:
                dstA, dbA = ka_ap[:, c0 - 1024:c0 - 512], buf("ka")
                dstB, dbB = kB_ap[:, c0 - 1024:c0 - 512], buf("bigA1")
            else:
                dstA, dbA = va_ap[:, c0 - 2048:c0 - 1536], buf("va")
                dstB = vB_ap[:, c0 - 2048:c0 - 1536]
                dbB = buf("bigA2") if si < 6 else buf("bigA3")
            vop(lambda e, dst=dstA, pbkA=pbkA, c0=c0: e.tensor_tensor(
                dst, pb[pbkA][:, 0:512], bias_big[:, c0:c0 + 512], ALU.add),
                reads=[pbufA, buf("bias_big")], writes=[dbA], tag="qkv_ev")
            vop(lambda e, dst=dstB, pbkB=pbkB, c0=c0: e.tensor_tensor(
                dst, pb[pbkB][:, 0:512], bias_big[:, c0:c0 + 512], ALU.add),
                reads=[pbufB, buf("bias_big")], writes=[dbB], tag="qkv_evB")
        tbA = xpos_tabs(chA * 128, N_PREF + chA * 128)
        xpos_inplace(qa_ap, buf("qa"), tab_ap(0), tab_ap(1), tbA[0:2])
        xpos_inplace(ka_ap, buf("ka"), tab_ap(2), tab_ap(3), tbA[2:4])
        tbB = xpos_tabs(chB * 128, N_PREF + chB * 128)
        xpos_inplace(qB_ap, buf("bigA0"), tab_ap(0), tab_ap(1), tbB[0:2])
        xpos_inplace(kB_ap, buf("bigA1"), tab_ap(2), tab_ap(3), tbB[2:4])
        retention_chunk(qa_ap, ka_ap, va_ap, buf("qa"), buf("ka"),
                        lambda h: buf("va"), rA_ap, buf("rA"))
        tedone([buf("qa"), buf("ka"), buf("va")])
        gn_chunk(chA, rA_ap, buf("rA"))
        retention_chunk(qB_ap, kB_ap, vB_ap, buf("bigA0"), buf("bigA1"),
                        lambda h: buf("bigA2") if h < 4 else buf("bigA3"),
                        rB_ap, buf("rB"))
        tedone([buf("bigA0"), buf("bigA1"), buf("bigA2"), buf("bigA3")])
        gn_chunk(chB, rB_ap, buf("rB"))
        for si in range(4):
            c0 = si * 512
            pbkA = si % 4
            pbkB = 4 + si % 4
            pbufA, pbufB = buf(f"pb{pbkA}"), buf(f"pb{pbkB}")
            slots = []
            for d in range(8):
                slot = wctr16[0] % 16
                wctr16[0] += 1
                slots.append((d, slot))
                wl = [buf(f"wbuf{slot}")]
                if slot >= 8:
                    wl = wfence(f"wb16F{slot}", wl)
                dma(wpiece16_ap(slot),
                    w_g[d * 128:(d + 1) * 128, c0:c0 + 512],
                    writes=wl, tag="wld",
                    queue=("sync", "scalar", "gpsimd")[d % 3])
            dmaflush("sync")
            dmaflush("scalar")
            dmaflush("gpsimd")
            for d, slot in slots:
                top(lambda e, d=d, pbkA=pbkA, slot=slot, cA=cA: e.matmul(
                    pb[pbkA][:, 0:512], xnfm_own_ap(d, cA), wpiece16_ap(slot),
                    start=(d == 0), stop=(d == 7)),
                    reads=[buf(f"wbuf{slot}"), xnfm_b], writes=[pbufA],
                    tag="wg_mm")
                top(lambda e, d=d, pbkB=pbkB, slot=slot, cB=cB: e.matmul(
                    pb[pbkB][:, 0:512], xnfm_own_ap(d, cB), wpiece16_ap(slot),
                    start=(d == 0), stop=(d == 7)),
                    reads=[buf(f"wbuf{slot}"), xnfm_b], writes=[pbufB],
                    tag="wg_mmB")
            tedone([buf(f"wbuf{sl}") for _, sl in slots])
            for pk, tg, rbuf, r_ap, pbuf in (
                    (pbkA, 0, buf("rA"), rA_ap, pbufA),
                    (pbkB, 1, buf("rB"), rB_ap, pbufB)):
                tg_ap = tmpg[tg]
                tb2 = buf(f"tmpg{tg}")
                vop(lambda e, pk=pk, c0=c0, tg_ap=tg_ap: e.tensor_tensor(
                    tg_ap, pb[pk][:, 0:512], bg_s[:, c0:c0 + 512], ALU.add),
                    reads=[pbuf, buf("c_bg")], writes=[tb2], tag="wg_ev")
                aop(lambda e, tg_ap=tg_ap: e.activation(tg_ap, tg_ap, AF.Silu),
                    reads=[tb2], writes=[tb2], tag="wg_silu")
                vop(lambda e, r_ap=r_ap, c0=c0, tg_ap=tg_ap: e.tensor_tensor(
                    r_ap[:, c0:c0 + 512], tg_ap, r_ap[:, c0:c0 + 512],
                    ALU.mult),
                    reads=[tb2, rbuf], writes=[rbuf], tag="gate_mul")
        tedone([xnfm_b])
        if pr < 7:
            pair_prologue(pr + 1)
        for v0 in range(0, 16, 4):
            transpose_batch4(
                [rA_ap[:, (v0 + j) * 128:(v0 + j + 1) * 128] for j in range(4)],
                [buf("rA")],
                bigC[:, 4096 + v0 * 128:4096 + (v0 + 4) * 128],
                buf("gfm"), 4 + (v0 // 4) % 2)
        for v0 in range(0, 16, 4):
            transpose_batch4(
                [rB_ap[:, (v0 + j) * 128:(v0 + j + 1) * 128] for j in range(4)],
                [buf("rB")],
                rB_ap[:, v0 * 128:(v0 + 4) * 128],
                buf("rB"), 6 + (v0 // 4) % 2)
        for si in range(2):
            c0 = si * 512
            pbufA, pbufB = buf(f"pb{si}"), buf(f"pb{2 + si}")
            for vb in (0, 8):
                slots = []
                for vv in range(8):
                    vc = vb + vv
                    slot = wctr16[0] % 16
                    wctr16[0] += 1
                    slots.append((vc, slot))
                    wl = [buf(f"wbuf{slot}")]
                    if slot >= 8:
                        wl = wfence(f"wb16F{slot}", wl)
                    dma(wpiece16_ap(slot),
                        w_p[vc * 128:(vc + 1) * 128, c0:c0 + 512],
                        writes=wl, tag="wld",
                        queue=("sync", "scalar", "gpsimd")[vv % 3])
                dmaflush("sync")
                dmaflush("scalar")
                dmaflush("gpsimd")
                for vc, slot in slots:
                    top(lambda e, vc=vc, si=si, slot=slot: e.matmul(
                        pb[si][:, 0:512], gfm_ap(vc), wpiece16_ap(slot),
                        start=(vc == 0), stop=(vc == 15)),
                        reads=[buf(f"wbuf{slot}"), buf("gfm")], writes=[pbufA],
                        tag="wp_mm")
                    top(lambda e, vc=vc, si=si, slot=slot: e.matmul(
                        pb[2 + si][:, 0:512],
                        rB_ap[:, vc * 128:(vc + 1) * 128], wpiece16_ap(slot),
                        start=(vc == 0), stop=(vc == 15)),
                        reads=[buf(f"wbuf{slot}"), buf("rB")], writes=[pbufB],
                        tag="wp_mmB")
                tedone([buf(f"wbuf{sl}") for _, sl in slots])
            vop(lambda e, si=si, c0=c0: e.tensor_tensor(
                stile_ap()[:, c0:c0 + 512], pb[si][:, 0:512],
                bp_s[:, c0:c0 + 512], ALU.add),
                reads=[pbufA, buf("c_bp")], writes=[buf("stile")], tag="wp_ev")
            vop(lambda e, si=si, c0=c0: e.tensor_tensor(
                stileB_ap[:, c0:c0 + 512], pb[2 + si][:, 0:512],
                bp_s[:, c0:c0 + 512], ALU.add),
                reads=[pbufB, buf("c_bp")], writes=[buf("stileB")],
                tag="wp_evB")
        tedone([buf("gfm"), buf("rB")])
        dma(s_dram[chA * 128:(chA + 1) * 128, :], stile_ap(),
            reads=[buf("stile")], writes=[buf("sdram")], tag="s_st",
            queue="gpsimd")
        dma(s_dram[chB * 128:(chB + 1) * 128, :], stileB_ap,
            reads=[buf("stileB")], writes=[buf("sdram")], tag="s_st",
            queue="gpsimd")

    pair_prologue(0)
    for pr in range(8):
        owned_pair(pr)

    dma(bias_big[:], rb2_d[:], writes=[buf("bias_big")], tag="rb2ld",
        queue="sync")
    dmaflush("sync")

    def xnfm_half_ap(d, t0, n):
        return bigA[:, d * 1024 + t0:d * 1024 + t0 + n]

    def h1q_ap(f1c, t0, n):
        return bigB[:, f1c * 1024 + t0:f1c * 1024 + t0 + n]

    def w2q_ap(f1c):
        return bigC[:, f1c * 1024:(f1c + 1) * 1024]

    def gacc_ap(tc, c0, n):
        return bigD[:, tc * 1024 + c0:tc * 1024 + c0 + n]

    BF16 = mybir.dt.bfloat16
    medw_r = (medw[:], scr[:, 0:1024])
    medw2_r = (maskT[:, 0:1024], scr[:, 1024:2048])
    xnhi = bigA[:, 0:4096].bitcast(BF16)
    xnlo = bigA[:, 4096:8192].bitcast(BF16)
    h1hi = bigB[:, 0:4096].bitcast(BF16)
    h1lo = bigB[:, 4096:8192].bitcast(BF16)

    def w1hi_ap(ring):
        return med[:, ring * 1024:ring * 1024 + 512].bitcast(BF16)

    def w1lo_ap(ring):
        return med[:, ring * 1024 + 512:(ring + 1) * 1024].bitcast(BF16)

    def w2hi_ap(f1c):
        return bigC[:, f1c * 1024:f1c * 1024 + 512].bitcast(BF16)

    def w2lo_ap(f1c):
        return bigC[:, f1c * 1024 + 512:(f1c + 1) * 1024].bitcast(BF16)

    sld_ap = med[:, 2048:3072]
    xld_ap = med[:, 3072:4096]
    xg_ap = med[:, 4096:5120]

    EXPERTS = [("s", 0, sw1, sw2), ("s", 1, sw1, sw2),
               ("r", 0, rw1, rw2), ("r", 1, rw1, rw2),
               ("r", 2, rw1, rw2), ("r", 3, rw1, rw2)]

    ffn_fence_done = [False]

    def ffn_half(hf):
        tok0 = hf * 1024
        for tc in range(8):
            ch = hf * 8 + tc
            dma(xld_ap, x_in[N_PREF + tok0 + tc * 128:
                             N_PREF + tok0 + (tc + 1) * 128, :],
                writes=wfence("medS1", [buf("xld")]), tag="xld2",
                queue="gpsimd")
            dmaflush("gpsimd")
            vop(lambda e, ch=ch: e.tensor_scalar(
                xg_ap, xld_ap, stats_s[:, 2 * ch:2 * ch + 1],
                stats_s[:, 2 * ch + 1:2 * ch + 2], ALU.add, ALU.mult),
                reads=[buf("xld"), buf("stats")],
                writes=wfence("medXG", [buf("xg")]),
                tag="ln_reapply")
            xnhi3 = xnhi.rearrange("p (j f) -> p j f", f=1024)
            xnlo3 = xnlo.rearrange("p (j f) -> p j f", f=1024)
            first = (tc == 0 and not ffn_fence_done[0])
            if first:
                ffn_fence_done[0] = True
            for j0, pbk in ((0, 4), (4, 5)):
                pbuf = buf(f"pb{pbk}")
                for j in range(4):
                    top(lambda e, j0=j0, j=j, pbk=pbk: e.transpose(
                        pb[pbk][:, j * 128:(j + 1) * 128],
                        xg_ap[:, (j0 + j) * 128:(j0 + j + 1) * 128],
                        ident[:, 0:128]),
                        reads=[buf("xg"), buf("c_ident")], writes=[pbuf],
                        tag="x2tr")
                ps3 = pb[pbk].rearrange("p (j f) -> p j f", f=128)
                vop(lambda e, j0=j0, tc=tc, ps3=ps3: e.tensor_copy(
                    xnhi3[:, j0:j0 + 4, tc * 128:(tc + 1) * 128], ps3),
                    reads=[pbuf],
                    writes=wfence("bigA", [buf("xnA")])
                    if (first and j0 == 0) else [buf("xnA")],
                    tag="x2tr_ev")
                vop(lambda e, j0=j0, tc=tc, ps3=ps3: e.tensor_tensor(
                    xnlo3[:, j0:j0 + 4, tc * 128:(tc + 1) * 128], ps3,
                    xnhi3[:, j0:j0 + 4, tc * 128:(tc + 1) * 128], ALU.subtract),
                    reads=[pbuf, buf("xnA")], writes=[buf("xnA")],
                    tag="x2tr_lo")
        for ei, (kind, en, w1d, w2d) in enumerate(EXPERTS):
            for q in range(4):
                f10 = q * 1024

                def emit_wdma(f1c, f10=f10, en=en, w1d=w1d, w2d=w2d):
                    ring = f1c % 2
                    f1a = f10 + f1c * 128
                    src_w1 = (w1d[en, :, f1a:f1a + 128]
                              .rearrange("(j p) f -> p j f", p=128))
                    qq = wqueue()
                    qq2 = wqueue()
                    mw = medw_r[ring]
                    mw2 = medw2_r[ring]
                    dma(mw.rearrange("p (j f) -> p j f", j=8), src_w1,
                        writes=wfence(f"medwF{ring}", [buf(f"medw{ring}")]),
                        tag="w1ld", queue=qq)
                    dmaflush(qq)
                    dma(mw2, w2d[en, f1a:f1a + 128, :],
                        writes=wfence(f"medw2F{ring}",
                                      [buf(f"medw2{ring}")]),
                        tag="w2ld", queue=qq2)
                    dmaflush(qq2)

                def emit_wsplit(f1c, ei=ei, q=q):
                    ring = f1c % 2
                    w1b = buf(f"w1pc{ring}")
                    w2b = buf(f"w2q{f1c}")
                    mw = medw_r[ring]
                    mw2 = medw2_r[ring]
                    mwb = buf(f"medw{ring}")
                    mw2b = buf(f"medw2{ring}")
                    aop(lambda e, ring=ring, mw=mw: e.activation(
                        w1hi_ap(ring), mw, AF.Identity),
                        reads=[mwb],
                        writes=wfence("medS0", [w1b]), tag="w1hi")
                    vop(lambda e, ring=ring, mw=mw: e.tensor_tensor(
                        w1lo_ap(ring), mw, w1hi_ap(ring), ALU.subtract),
                        reads=[mwb, w1b], writes=[w1b], tag="w1lo")
                    aop(lambda e, f1c=f1c, mw2=mw2: e.activation(
                        w2hi_ap(f1c), mw2, AF.Identity),
                        reads=[mw2b],
                        writes=wfence("bigC", [w2b]) if (ei == 0 and q == 0)
                        else [w2b],
                        tag="w2hi")
                    vop(lambda e, f1c=f1c, mw2=mw2: e.tensor_tensor(
                        w2lo_ap(f1c), mw2, w2hi_ap(f1c), ALU.subtract),
                        reads=[mw2b, w2b], writes=[w2b], tag="w2lo")

                emit_wdma(0)
                emit_wsplit(0)
                for f1c in range(8):
                    ring = f1c % 2
                    w1b = buf(f"w1pc{ring}")
                    jcol = (f10 + f1c * 128) // 128
                    if f1c < 7:
                        emit_wdma(f1c + 1)
                    if kind == "s":
                        bcol = sb1c_s[:, en * 32 + jcol:en * 32 + jcol + 1]
                        bbuf = buf("c_sb1c")
                    else:
                        bcol = rb1c_s[:, en * 32 + jcol:en * 32 + jcol + 1]
                        bbuf = buf("c_rb1c")
                    for th in range(2):
                        t0c = th * 512
                        pbk = th
                        pbuf = buf(f"pb{pbk}")
                        nmm = 24
                        im = 0
                        for d in range(8):
                            for wap, xap in (
                                (w1hi_ap(ring), xnhi),
                                (w1hi_ap(ring), xnlo),
                                (w1lo_ap(ring), xnhi),
                            ):
                                top(lambda e, d=d, wap=wap, xap=xap, t0c=t0c,
                                    pbk=pbk, im=im, nmm=nmm: e.matmul(
                                        pb[pbk][:, 0:512],
                                        wap[:, d * 128:(d + 1) * 128],
                                        xap[:, d * 1024 + t0c:
                                            d * 1024 + t0c + 512],
                                        start=(im == 0), stop=(im == nmm - 1)),
                                    reads=[w1b, buf("xnA")], writes=[pbuf],
                                    tag="h1_mm")
                                im += 1
                        if th == 1 and f1c < 7:
                            emit_wsplit(f1c + 1)
                        tbk = 6 + th
                        tbuf = buf(f"pb{tbk}")
                        aop(lambda e, f1c=f1c, t0c=t0c, pbk=pbk, bcol=bcol:
                            e.activation(
                                h1hi[:, f1c * 1024 + t0c:f1c * 1024 + t0c + 512],
                                pb[pbk][:, 0:512], AF.Relu, bias=bcol),
                            reads=[pbuf, bbuf],
                            writes=wfence("bigB", [buf("h1q")])
                            if (ei == 0 and q == 0 and f1c == 0 and th == 0)
                            else [buf("h1q")],
                            tag="h1_hi")
                        aop(lambda e, pbk=pbk, tbk=tbk, bcol=bcol: e.activation(
                            pb[tbk][:, 0:512], pb[pbk][:, 0:512], AF.Relu,
                            bias=bcol),
                            reads=[pbuf, bbuf], writes=[tbuf], tag="h1_relu")
                        vop(lambda e, f1c=f1c, t0c=t0c, tbk=tbk: e.tensor_tensor(
                            h1lo[:, f1c * 1024 + t0c:f1c * 1024 + t0c + 512],
                            pb[tbk][:, 0:512],
                            h1hi[:, f1c * 1024 + t0c:f1c * 1024 + t0c + 512],
                            ALU.subtract),
                            reads=[tbuf, buf("h1q")], writes=[buf("h1q")],
                            tag="h1_lo")
                    tedone([w1b])
                for tc in range(8):
                    ch = hf * 8 + tc
                    for si in range(2):
                        c0 = si * 512
                        pbk = 2 + si
                        pbuf = buf(f"pb{pbk}")
                        im = 0
                        nmm = 24
                        for f1c in range(8):
                            for hap, wap in (
                                (h1hi, w2hi_ap(f1c)),
                                (h1hi, w2lo_ap(f1c)),
                                (h1lo, w2hi_ap(f1c)),
                            ):
                                top(lambda e, f1c=f1c, tc=tc, c0=c0, pbk=pbk,
                                    hap=hap, wap=wap, im=im, nmm=nmm: e.matmul(
                                        pb[pbk][:, 0:512],
                                        hap[:, f1c * 1024 + tc * 128:
                                            f1c * 1024 + (tc + 1) * 128],
                                        wap[:, c0:c0 + 512],
                                        start=(im == 0), stop=(im == nmm - 1)),
                                    reads=[buf("h1q"), buf(f"w2q{f1c}")],
                                    writes=[pbuf], tag="h2_mm")
                                im += 1
                        gb = buf("gacc")
                        if ei == 0 and q == 0:
                            vop(lambda e, tc=tc, c0=c0, pbk=pbk: e.tensor_scalar(
                                gacc_ap(tc, c0, 512), pb[pbk][:, 0:512],
                                0.5, None, ALU.mult),
                                reads=[pbuf],
                                writes=wfence("bigD", [gb])
                                if (tc == 0 and si == 0) else [gb],
                                tag="h2_init")
                        elif kind == "s":
                            vop(lambda e, tc=tc, c0=c0, pbk=pbk:
                                e.scalar_tensor_tensor(
                                    gacc_ap(tc, c0, 512), pb[pbk][:, 0:512],
                                    0.5, gacc_ap(tc, c0, 512),
                                    ALU.mult, ALU.add),
                                reads=[pbuf, gb], writes=[gb], tag="h2_acc_s")
                        else:
                            ccol = cstore_s[:, ch * 4 + en:ch * 4 + en + 1]
                            vop(lambda e, tc=tc, c0=c0, pbk=pbk, ccol=ccol:
                                e.scalar_tensor_tensor(
                                    gacc_ap(tc, c0, 512), pb[pbk][:, 0:512],
                                    ccol, gacc_ap(tc, c0, 512),
                                    ALU.mult, ALU.add),
                                reads=[pbuf, gb, buf("cstore")], writes=[gb],
                                tag="h2_acc_r")
                tedone([buf(f"w2q{f1c}") for f1c in range(8)] + [buf("h1q")])
        tedone([buf("xnA")])
        for tc in range(8):
            ch = hf * 8 + tc
            gb = buf("gacc")
            for en in range(4):
                ccol = cstore_s[:, ch * 4 + en:ch * 4 + en + 1]
                vop(lambda e, en=en, ccol=ccol: e.tensor_scalar(
                    xg_ap, bias_big[:, en * E:(en + 1) * E], ccol, None,
                    ALU.mult),
                    reads=[buf("bias_big"), buf("cstore")], writes=[buf("xg")],
                    tag="rb2c")
                vop(lambda e, tc=tc: e.tensor_tensor(
                    gacc_ap(tc, 0, E), gacc_ap(tc, 0, E), xg_ap, ALU.add),
                    reads=[buf("xg"), gb], writes=[gb], tag="rb2add")
            vop(lambda e, tc=tc: e.tensor_tensor(
                gacc_ap(tc, 0, E), gacc_ap(tc, 0, E), sb2h_s[:, 0:E], ALU.add),
                reads=[buf("c_sb2h"), gb], writes=[gb], tag="sb2add")
            if hf == 0 and tc in (0, 5):
                tap(f"gpre{tc}", gacc_ap(tc, 0, E), [gb], 1024)
            aop(lambda e, tc=tc: e.activation(
                gacc_ap(tc, 0, E), gacc_ap(tc, 0, E), AF.Silu),
                reads=[gb], writes=[gb], tag="g_silu")
            dma(sld_ap, s_dram[ch * 128:(ch + 1) * 128, :],
                reads=[buf("sdram")], writes=[buf("sld")], tag="s_ld",
                queue="gpsimd")
            dma(xld_ap, x_in[N_PREF + ch * 128:N_PREF + (ch + 1) * 128, :],
                writes=[buf("xld")], tag="xld3", queue="gpsimd")
            dmaflush("gpsimd")
            vop(lambda e, tc=tc: e.tensor_tensor(
                xg_ap, gacc_ap(tc, 0, E), sld_ap, ALU.mult),
                reads=[gb, buf("sld")], writes=[buf("xg")], tag="fin_mul")
            vop(lambda e: e.tensor_tensor(xg_ap, xg_ap, xld_ap, ALU.add),
                reads=[buf("xg"), buf("xld")], writes=[buf("xg")], tag="fin_add")
            dma(out_d[ch * 128:(ch + 1) * 128, :], xg_ap,
                reads=[buf("xg")], writes=[buf("outd")], tag="out_st",
                queue="gpsimd")

    tap("cstore", cstore_s[:, 0:64], [buf("cstore")], 64)
    tap("stats", stats_s[:, 0:32], [buf("stats")], 32)

    for hf in range(2):
        ffn_half(hf)

    counts = em.finalize()
    per_eng = {e: [] for e in ENGINES}
    for o in em.ops:
        per_eng[o.eng].append(o)

    es = ExitStack()
    for key in [("d", "sync"), ("d", "gpsimd"), ("d", "scalar"),
                ("p", "tensor"), ("p", "vector"), ("p", "scalar"),
                ("p", "gpsimd"), ("p", "sync")]:
        sems[key] = es.enter_context(nc.semaphore(f"{key[0]}_{key[1]}"))
    final_dG = counts.get(("d", "gpsimd"), 0)

    with es:
        with nc.Block() as block:
            def make_body(eng_name):
                ops = per_eng[eng_name]

                def body(e):
                    hw = {}
                    for o in ops:
                        for dsel, _nw in o.deps:
                            if dsel.val is None:
                                continue
                            k = _sem_key(dsel)
                            v = dsel.val
                            if dsel.dma and DMA_SLACK:
                                v = min(v + DMA_SLACK, counts.get(k, v))
                            if hw.get(k, -1) >= v:
                                continue
                            hw[k] = v
                            e.wait_ge(sems[k], v)
                        instr = o.fn(e)
                        if o.dma:
                            instr.then_inc(sems[("d", o.eng)], 16)
                        else:
                            if o.small or o.need_inc:
                                instr = e.drain()
                            if o.need_inc:
                                if instr is None:
                                    instr = e.nop()
                                instr.then_inc(sems[("p", o.eng)], 1)
                    if eng_name == "gpsimd" and final_dG:
                        e.wait_ge(sems[("d", "gpsimd")], final_dG)
                return body

            for eng_name in ENGINES:
                getattr(block, eng_name)(make_body(eng_name))

    return nc, counts, len(em.ops)


def kernel(L0_attn_hidden_states, L0_attn_w_qkv, L0_attn_b_qkv, L0_attn_w_g,
           L0_attn_b_g, L0_attn_w_p, L0_attn_b_p, L0_ffn_gate_w, L0_ffn_gate_b,
           L0_ffn_sw1, L0_ffn_sb1, L0_ffn_sw2, L0_ffn_sb2, L0_ffn_rw1,
           L0_ffn_rb1, L0_ffn_rw2, L0_ffn_rb2):
    x = np.ascontiguousarray(np.asarray(L0_attn_hidden_states, dtype=np.float32))
    wqkv = np.ascontiguousarray(np.asarray(L0_attn_w_qkv, dtype=np.float32))
    bqkv = np.asarray(L0_attn_b_qkv, dtype=np.float32)
    wg = np.ascontiguousarray(np.asarray(L0_attn_w_g, dtype=np.float32))
    bg = np.asarray(L0_attn_b_g, dtype=np.float32)
    wp = np.ascontiguousarray(np.asarray(L0_attn_w_p, dtype=np.float32))
    bp = np.asarray(L0_attn_b_p, dtype=np.float32)
    gw = np.asarray(L0_ffn_gate_w, dtype=np.float32)
    gb = np.asarray(L0_ffn_gate_b, dtype=np.float32)
    sw1 = np.ascontiguousarray(np.asarray(L0_ffn_sw1, dtype=np.float32))
    sb1 = np.asarray(L0_ffn_sb1, dtype=np.float32)
    sw2 = np.ascontiguousarray(np.asarray(L0_ffn_sw2, dtype=np.float32))
    sb2 = np.asarray(L0_ffn_sb2, dtype=np.float32)
    rw1 = np.ascontiguousarray(np.asarray(L0_ffn_rw1, dtype=np.float32))
    rb1 = np.asarray(L0_ffn_rb1, dtype=np.float32)
    rw2 = np.ascontiguousarray(np.asarray(L0_ffn_rw2, dtype=np.float32))
    rb2 = np.asarray(L0_ffn_rb2, dtype=np.float32)

    nc, _, _ = build_nc()

    bqkv_bc = np.ascontiguousarray(np.broadcast_to(bqkv[None, :], (128, 4 * E)))
    bg_bc = np.ascontiguousarray(np.broadcast_to(bg[None, :], (128, VV)))
    bp_bc = np.ascontiguousarray(np.broadcast_to(bp[None, :], (128, E)))
    gateb_bc = np.ascontiguousarray(np.broadcast_to(gb[None, :], (128, 4)))
    gatew_sb = np.ascontiguousarray(gw.reshape(8, 128, 4).transpose(1, 0, 2)
                                    .reshape(128, 32))
    sb1c = np.ascontiguousarray(sb1.reshape(2, 32, 128).transpose(2, 0, 1)
                                .reshape(128, 64))
    rb1c = np.ascontiguousarray(rb1.reshape(4, 32, 128).transpose(2, 0, 1)
                                .reshape(128, 128))
    sb2h_bc = np.ascontiguousarray(np.broadcast_to(
        (0.5 * (sb2[0].astype(np.float64) + sb2[1].astype(np.float64)))
        .astype(np.float32)[None, :], (128, E)))
    rb2_bc = np.ascontiguousarray(np.broadcast_to(rb2[None, :, :], (128, 4, E))
                                  .reshape(128, 4 * E))
    magic = np.ascontiguousarray(np.broadcast_to(
        np.array([0x5F3759DF, 0x7EF477D5], dtype=np.uint32)
        .view(np.float32)[None, :], (128, 2)))

    in_maps = []
    for core in range(NCORES):
        b = core // 4
        t0 = (core % 4) * N_OWN
        npad = max(0, N_PREF - t0)
        xpad = np.zeros((npad, E), dtype=np.float32)
        xpre = x[b, max(0, t0 - N_PREF):t0]
        xcore = np.ascontiguousarray(
            np.concatenate([xpad, xpre, x[b, t0:t0 + N_OWN]], axis=0))
        cc = _host_constants(core)
        in_maps.append({
            "x": xcore, "w_qkv": wqkv, "w_g": wg, "w_p": wp,
            "sw1": sw1, "sw2": sw2, "rw1": rw1, "rw2": rw2,
            "ident": cc["ident"], "maskT": cc["maskT"],
            "rho": cc["rho"], "rho2": cc["rho2"], "gpow": cc["gpow"],
            "dec_own": cc["dec_own"], "dec_pref": cc["dec_pref"],
            "cosq": cc["cosq"], "Bq": cc["Bq"],
            "cosk": cc["cosk"], "Bk": cc["Bk"],
            "bqkv_bc": bqkv_bc, "bg_bc": bg_bc, "bp_bc": bp_bc,
            "gateb_bc": gateb_bc, "gatew_sb": gatew_sb,
            "sb1c": sb1c, "rb1c": rb1c, "sb2h_bc": sb2h_bc, "rb2_bc": rb2_bc,
            "magic": magic,
        })

    res = run_bass_kernel_spmd(nc, in_maps, core_ids=list(range(NCORES)))
    out = np.empty((B, T, E), dtype=np.float32)
    for core in range(NCORES):
        b = core // 4
        t0 = (core % 4) * N_OWN
        out[b, t0:t0 + N_OWN] = res.results[core]["out"]
    return out
